# version 07
# baseline (speedup 1.0000x reference)
import numpy as np
from contextlib import ExitStack

import concourse.bass as bass
import concourse.mybir as mybir
from concourse.bass_utils import run_bass_kernel_spmd

F32 = mybir.dt.float32
AF = mybir.ActivationFunctionType
OP = mybir.AluOpType

D = 2048
T = 1024
KC = 16
NG = 2
NT = 512
E = 8
DIN = 4096
CONVD = 4352
PROJ = 8512
PROJ_PAD = 8576
NH = 64
P = 64
NSTATE = 128
CS = 256
NCH = 4
EPS = 1e-5



def _relayout_w(w):
    ki, mo = w.shape
    a = w.reshape(ki // 128, 128, mo // 128, 128).transpose(2, 1, 0, 3)
    return np.ascontiguousarray(a.reshape(mo // 128, 128, ki))


def _cols(v):
    n = v.shape[0] // 128
    return np.ascontiguousarray(v.reshape(n, 128).T)


def prep_inputs(inp):
    f32 = np.float32
    sh = {}
    sh["wr_s"] = _relayout_w(np.asarray(inp["L0_attn_Wr"], f32))
    sh["wf_s"] = _relayout_w(np.asarray(inp["L0_attn_Wf"], f32))
    sh["wc_s"] = _relayout_w(np.asarray(inp["L0_attn_Wc"], f32))
    sh["wo_s"] = _relayout_w(np.asarray(inp["L0_attn_Wo"], f32))
    gk = np.asarray(inp["L0_ffn_gate_kernel"], f32)
    sh["gk_s"] = np.ascontiguousarray(gk.reshape(KC, 128, E).transpose(1, 0, 2))
    sh["lnscale_s"] = _cols(np.asarray(inp["L0_attn_ln_scale"], f32))
    sh["wi0_s"] = np.stack([_relayout_w(np.asarray(inp["L0_ffn_wi_0"], f32)[e]) for e in range(E)])
    sh["wi1_s"] = np.stack([_relayout_w(np.asarray(inp["L0_ffn_wi_1"], f32)[e]) for e in range(E)])
    sh["wom_s"] = np.stack([_relayout_w(np.asarray(inp["L0_ffn_wo"], f32)[e]) for e in range(E)])
    win = np.asarray(inp["L1_attn_W_in"], f32)
    winT = np.zeros((D, PROJ_PAD), f32)
    winT[:, :PROJ] = win.T
    sh["win_s"] = _relayout_w(winT)
    sh["wout_s"] = _relayout_w(np.ascontiguousarray(np.asarray(inp["L1_attn_W_out"], f32).T))
    cwv = np.asarray(inp["L1_attn_conv_w"], f32).reshape(CONVD, 4)
    sh["convw_s"] = np.ascontiguousarray(cwv.reshape(34, 128, 4).transpose(1, 0, 2).reshape(128, 136))
    sh["convb_s"] = _cols(np.asarray(inp["L1_attn_conv_b"], f32))
    sh["dtbias_s"] = np.asarray(inp["L1_attn_dt_bias"], f32).reshape(NH, 1)
    sh["alog_s"] = np.asarray(inp["L1_attn_A_log"], f32).reshape(NH, 1)
    sh["dparam_s"] = np.asarray(inp["L1_attn_D_param"], f32).reshape(NH, 1)
    sh["normw_s"] = _cols(np.asarray(inp["L1_attn_norm_w"], f32))
    pw = np.asarray(inp["L1_ffn_pweight"], f32).transpose(1, 0, 2)
    nw = np.asarray(inp["L1_ffn_nweight"], f32).transpose(1, 0, 2)
    sh["pT_s"] = np.ascontiguousarray(pw.reshape(KC, 128, D * 4))
    sh["nT_s"] = np.ascontiguousarray(nw.reshape(KC, 128, D * 4))
    sh["exps_s"] = np.asarray(inp["L1_ffn_exps"], f32).reshape(1, 4)
    sh["bexps_s"] = np.asarray(inp["L1_ffn_bexps"], f32).reshape(1, 4)
    sh["maskw_s"] = np.asarray(inp["L1_ffn_mask_weight"], f32).reshape(1, 4)
    sh["scale_s"] = np.asarray(inp["L1_ffn_scale"], f32).reshape(1, 1)
    sh["bscale_s"] = np.asarray(inp["L1_ffn_biasscale"], f32).reshape(1, 1)
    pb = np.asarray(inp["L1_ffn_pbias"], f32)
    nb = np.asarray(inp["L1_ffn_nbias"], f32)
    sh["pbias_s"] = np.ascontiguousarray(pb.reshape(KC, 128, 4).transpose(1, 0, 2).reshape(128, KC * 4))
    sh["nbias_s"] = np.ascontiguousarray(nb.reshape(KC, 128, 4).transpose(1, 0, 2).reshape(128, KC * 4))
    sh["identity_c"] = np.eye(128, dtype=f32)
    l_idx = np.arange(CS)[None, :]
    s_idx = np.arange(128)[:, None]
    sh["mask0_c"] = (l_idx >= s_idx).astype(f32)
    sh["mask1_c"] = (l_idx >= s_idx + 128).astype(f32)

    x = np.asarray(inp["L0_attn_x"], f32)
    percore = []
    for c in range(8):
        bb, hh = c // 2, c % 2
        xs = np.ascontiguousarray(x[bb, hh * T:(hh + 1) * T, :].T)
        sel = np.zeros((128, 8), f32)
        if hh == 1:
            sel[:, c - 1] = 1.0
        par = np.full((128, 1), 1.0 if hh == 0 else 0.0, f32)
        percore.append({"x_fm": xs, "sel_v": sel, "par_v": par})
    return sh, percore



class Sem:
    def __init__(self, stack, nc, name):
        self.h = stack.enter_context(nc.semaphore(name))
        self.n = 0


class Builder:
    def __init__(self, stage="full"):
        self.stage = stage
        nc = bass.Bass("TRN2", target_bir_lowering=False, debug=False,
                       num_devices=8)
        self.nc = nc
        self.st = ExitStack()
        st = self.st
        dr = lambda n, s, kind="Internal": nc.dram_tensor(n, s, F32, kind=kind).ap()
        self.x_fm = dr("x_fm", [D, T], "ExternalInput")
        self.sel_v = dr("sel_v", [128, 8], "ExternalInput")
        self.par_v = dr("par_v", [128, 1], "ExternalInput")
        self.wr_s = dr("wr_s", [KC, 128, D], "ExternalInput")
        self.wf_s = dr("wf_s", [KC, 128, D], "ExternalInput")
        self.wc_s = dr("wc_s", [KC, 128, D], "ExternalInput")
        self.wo_s = dr("wo_s", [KC, 128, D], "ExternalInput")
        self.gk_s = dr("gk_s", [128, KC, E], "ExternalInput")
        self.lnscale_s = dr("lnscale_s", [128, KC], "ExternalInput")
        self.wi0_s = dr("wi0_s", [E, KC, 128, D], "ExternalInput")
        self.wi1_s = dr("wi1_s", [E, KC, 128, D], "ExternalInput")
        self.wom_s = dr("wom_s", [E, KC, 128, D], "ExternalInput")
        self.win_s = dr("win_s", [PROJ_PAD // 128, 128, D], "ExternalInput")
        self.wout_s = dr("wout_s", [KC, 128, DIN], "ExternalInput")
        self.convw_s = dr("convw_s", [128, 136], "ExternalInput")
        self.convb_s = dr("convb_s", [128, 34], "ExternalInput")
        self.dtbias_s = dr("dtbias_s", [NH, 1], "ExternalInput")
        self.alog_s = dr("alog_s", [NH, 1], "ExternalInput")
        self.dparam_s = dr("dparam_s", [NH, 1], "ExternalInput")
        self.normw_s = dr("normw_s", [128, DIN // 128], "ExternalInput")
        self.pT_s = dr("pT_s", [KC, 128, D * 4], "ExternalInput")
        self.nT_s = dr("nT_s", [KC, 128, D * 4], "ExternalInput")
        self.exps_s = dr("exps_s", [1, 4], "ExternalInput")
        self.bexps_s = dr("bexps_s", [1, 4], "ExternalInput")
        self.maskw_s = dr("maskw_s", [1, 4], "ExternalInput")
        self.scale_s = dr("scale_s", [1, 1], "ExternalInput")
        self.bscale_s = dr("bscale_s", [1, 1], "ExternalInput")
        self.pbias_s = dr("pbias_s", [128, KC * 4], "ExternalInput")
        self.nbias_s = dr("nbias_s", [128, KC * 4], "ExternalInput")
        self.identity_c = dr("identity_c", [128, 128], "ExternalInput")
        self.mask0_c = dr("mask0_c", [128, CS], "ExternalInput")
        self.mask1_c = dr("mask1_c", [128, CS], "ExternalInput")
        self.out = dr("out", [D, T], "ExternalOutput")
        self.d_r = dr("d_r", [KC, 128, T])
        self.d_u = dr("d_u", [KC, 128, T])
        self.d_xn = dr("d_xn", [KC, 128, T])
        self.d_s = dr("d_s", [KC, 128, T])
        self.d_g = dr("d_g", [KC, 128, T])
        self.d_x1 = dr("d_x1", [KC, 128, T])
        self.d_z = dr("d_z", [DIN // 128, 128, T])
        self.d_xbc = dr("d_xbc", [34, 128, T])
        self.d_xs = dr("d_xs", [DIN // 128, 128, T])
        self.d_yp = dr("d_yp", [DIN // 128, 128, T])
        self.d_g1 = dr("d_g1", [DIN // 128, 128, T])
        self.d_gl1 = dr("d_gl1", [KC, 128, T])
        self.cc1_src = dr("cc1_src", [128, KC])
        self.cc1_dst = dr("cc1_dst", [8, 128, KC])
        self.cc2_src = dr("cc2_src", [128, 34 * 3])
        self.cc2_dst = dr("cc2_dst", [8, 128, 34 * 3])
        self.cc3_src = dr("cc3_src", [128, NH * P])
        self.cc3_dst = dr("cc3_dst", [8, 128, NH * P])

        sb = lambda n, s: st.enter_context(nc.sbuf_tensor(n, s, F32))
        self.A = sb("A", [128, KC * T])
        self.B = sb("B", [128, KC * T])
        self.C4 = sb("C4", [128, 4224])
        self.W2 = sb("W2", [128, 2 * D])
        self.W4 = sb("W4", [128, 2 * D])
        self.M2 = sb("M2", [128, 2048])
        self.SM = sb("SM", [128, 1400])
        self.SM2 = sb("SM2", [NH, 2 * T + CS + 64])
        self.cwrow = self.SM2[0:E, T:2 * T]
        self.ones1 = sb("ones1", [1, 128])
        self.onesc = sb("onesc", [128, 1])
        self.row1 = sb("row1", [1, T])
        self.row2 = sb("row2", [1, T])

        ps = lambda n: st.enter_context(nc.psum_tensor(n, [128, NT], F32))
        self.PS = [ps(f"ps{i}") for i in range(8)]

        s = lambda n: Sem(st, nc, n)
        self.sW = s("sW")
        self.sG = s("sG")
        self.sT = s("sT")
        self.sV = s("sV")
        self.sS = s("sS")
        self.sC = s("sC")

        self.blk = st.enter_context(nc.Block())
        self.mb = self.M2[:, 0:T]
        self.rsb = self.M2[:, T:2 * T]

    def a_(self, k):
        return self.A[:, k * T:(k + 1) * T]

    def b_(self, k):
        return self.B[:, k * T:(k + 1) * T]

    def dma(self, eng, out, in_, sem):
        eng.dma_start(out=out, in_=in_).then_inc(sem.h, 16)
        sem.n += 16
        return sem.n

    def close(self):
        self.st.close()



def sm_layout(b):
    b.sm_off = {}
    off = 0

    def alloc(name, w):
        nonlocal off
        b.sm_off[name] = off
        off += w
        assert off <= 1400, (name, off)
    alloc("ident", 128)
    alloc("lnscale", KC)
    alloc("convw", 136)
    alloc("convb", 34)
    alloc("mask0", CS)
    alloc("mask1", CS)
    alloc("sel", 8)
    alloc("par", 1)
    alloc("c0", KC)
    alloc("hin1", KC)
    alloc("hin2", KC)
    alloc("bnd", KC)
    alloc("normw", DIN // 128)
    alloc("cvec", 4)
    alloc("ncvec", 4)
    alloc("bias1", KC)
    alloc("pbias", KC * 4)
    alloc("nbias", KC * 4)
    alloc("braw", KC)
    alloc("btmp", KC)
    alloc("btmp2", KC)
    alloc("bsc", 1)
    alloc("halo", 102)
    alloc("halo_rx", 102)
    alloc("dcols", 32)

    def sm(name, w=1):
        o = b.sm_off[name]
        return b.SM[:, o:o + w]
    b.sm = sm
    b.smc = lambda name, i: b.SM[:, b.sm_off[name] + i:b.sm_off[name] + i + 1]

    b.sm2_off = {}
    off2 = 0

    def alloc2(name, w):
        nonlocal off2
        b.sm2_off[name] = off2
        off2 += w
        assert off2 <= 2 * T + CS + 64, (name, off2)
    alloc2("dt", T)
    alloc2("acum", T)
    alloc2("zeros", CS)
    alloc2("A", 1)
    alloc2("dtbias", 1)
    alloc2("dparam", 1)
    alloc2("dalast", 4)
    alloc2("efac", 4)

    def sm2(name, w=1):
        o = b.sm2_off[name]
        return b.SM2[:, o:o + w]
    b.sm2 = sm2


def emit_mm_dxd(b, w_strips, x_tile_fn, evac, mo_list=None, nk=KC,
                ntok=NT, ngroups=NG, extra_tensor_waits=(), wbufs=None):
    if mo_list is None:
        mo_list = list(range(KC))
    if wbufs is None:
        wbufs = (b.W2[:, 0:nk * 128], b.W2[:, D:D + nk * 128])
    w0 = b.sW.n
    t0 = b.sT.n
    sem_e, e_base = evac

    def sync_body(e):
        for mi, m in enumerate(mo_list):
            e.wait_ge(b.sT.h, t0 + max(0, (mi - 1) * ngroups))
            b.dma(e, wbufs[mi % 2], w_strips(m), b.sW)
    b.blk.sync(sync_body)

    def tensor_body(e):
        for sem, val in extra_tensor_waits:
            e.wait_ge(sem.h, val)
        e.wait_ge(sem_e.h, e_base)
        for mi, m in enumerate(mo_list):
            e.wait_ge(b.sW.h, w0 + 16 * (mi + 1))
            for g in range(ngroups):
                psi = (mi % 2) * 2 + g
                if mi >= 2:
                    e.wait_ge(sem_e.h, e_base + (mi - 2) * ngroups + g + 1)
                ins = None
                for k in range(nk):
                    ins = e.matmul(b.PS[psi][:, :ntok],
                                   wbufs[mi % 2][:, k * 128:k * 128 + 128],
                                   x_tile_fn(k, g), start=(k == 0), stop=(k == nk - 1))
                ins.then_inc(b.sT.h, 1)
                b.sT.n += 1
    b.blk.tensor(tensor_body)
    return t0



def emit_preamble(b):
    blk = b.blk
    sm_layout(b)

    def sync_body(e):
        b.dma(e, b.sm("ident", 128), b.identity_c, b.sW)
        b.dma(e, b.sm("lnscale", KC), b.lnscale_s, b.sW)
        b.dma(e, b.sm("convw", 136), b.convw_s, b.sW)
        b.dma(e, b.sm("convb", 34), b.convb_s, b.sW)
        b.dma(e, b.sm("mask0", CS), b.mask0_c, b.sW)
        b.dma(e, b.sm("mask1", CS), b.mask1_c, b.sW)
        b.dma(e, b.sm("sel", 8), b.sel_v, b.sW)
        b.dma(e, b.sm("par", 1), b.par_v, b.sW)
        b.dma(e, b.sm("normw", DIN // 128), b.normw_s, b.sW)
        b.dma(e, b.sm("pbias", KC * 4), b.pbias_s, b.sW)
        b.dma(e, b.sm("nbias", KC * 4), b.nbias_s, b.sW)
        b.dma(e, b.sm2("A"), b.alog_s, b.sW)
        b.dma(e, b.sm2("dtbias"), b.dtbias_s, b.sW)
        b.dma(e, b.sm2("dparam"), b.dparam_s, b.sW)
        b.dma(e, b.row1[:, 0:4], b.exps_s, b.sW)
        b.dma(e, b.row1[:, 4:8], b.bexps_s, b.sW)
        b.dma(e, b.row1[:, 8:12], b.maskw_s, b.sW)
        b.dma(e, b.row1[:, 12:13], b.scale_s, b.sW)
        b.dma(e, b.row1[:, 13:14], b.bscale_s, b.sW)
    blk.sync(sync_body)
    w_done = b.sW.n

    def vector_ones(e):
        e.memset(b.ones1[:], 1.0)
        e.memset(b.onesc[:], 1.0)
        e.memset(b.sm2("zeros", CS), 0.0).then_inc(b.sV.h, 1)
        b.sV.n += 1
    blk.vector(vector_ones)
    b.v_ones = b.sV.n

    def scalar_a(e):
        e.wait_ge(b.sW.h, w_done)
        e.activation(out=b.sm2("A"), in_=b.sm2("A"), func=AF.Exp)
        e.mul(out=b.sm2("A"), in_=b.sm2("A"), mul=-1.0)
        e.activation(out=b.row1[:, 16:20], in_=b.row1[:, 8:12],
                     func=AF.Sigmoid).then_inc(b.sS.h, 1)
        b.sS.n += 1
    blk.scalar(scalar_a)
    s_a = b.sS.n

    def vector_cv(e):
        e.wait_ge(b.sS.h, s_a)
        e.tensor_tensor(out=b.row1[:, 20:24], in0=b.row1[:, 0:4],
                        in1=b.row1[:, 16:20], op=OP.mult)
        e.drain()
        e.tensor_scalar(out=b.row1[:, 20:24], in0=b.row1[:, 20:24],
                        scalar1=b.row1[:, 12:13], scalar2=None,
                        op0=OP.mult).then_inc(b.sV.h, 1)
        b.sV.n += 1
    blk.vector(vector_cv)
    v_cv = b.sV.n

    def tensor_bc(e):
        e.wait_ge(b.sV.h, v_cv)
        e.matmul(b.PS[0][:, 0:4], b.ones1[:], b.row1[0:1, 20:24], start=True, stop=True)
        e.matmul(b.PS[0][:, 4:8], b.ones1[:], b.row1[0:1, 4:8], start=False, stop=True)
        e.matmul(b.PS[0][:, 8:9], b.ones1[:], b.row1[0:1, 13:14],
                 start=False, stop=True).then_inc(b.sT.h, 1)
        b.sT.n += 1
    blk.tensor(tensor_bc)
    t_bc = b.sT.n

    def vector_bias(e):
        e.wait_ge(b.sT.h, t_bc)
        e.tensor_copy(out=b.sm("cvec", 4), in_=b.PS[0][:, 0:4])
        e.tensor_scalar(out=b.sm("ncvec", 4), in0=b.sm("cvec", 4), scalar1=-1.0,
                        scalar2=None, op0=OP.mult)
        e.tensor_copy(out=b.sm("bsc", 1), in_=b.PS[0][:, 8:9])
        pb = b.sm("pbias", KC * 4).rearrange("p (k n) -> p k n", n=4)
        nb = b.sm("nbias", KC * 4).rearrange("p (k n) -> p k n", n=4)
        braw = b.sm("braw", KC)
        btmp = b.sm("btmp", KC)
        e.memset(braw, 0.0)
        e.drain()
        for n_ in range(4):
            e.tensor_tensor(out=btmp, in0=pb[:, :, n_], in1=nb[:, :, n_], op=OP.subtract)
            e.drain()
            e.scalar_tensor_tensor(out=braw, in0=btmp, scalar=b.PS[0][:, 4 + n_:5 + n_],
                                   in1=braw, op0=OP.mult, op1=OP.add)
            e.drain()
        e.tensor_scalar(out=braw, in0=braw, scalar1=-1.0, scalar2=1.0,
                        op0=OP.max, op1=OP.min).then_inc(b.sV.h, 1)
        b.sV.n += 1
    blk.vector(vector_bias)
    v_braw = b.sV.n

    def scalar_sgn(e):
        e.wait_ge(b.sV.h, v_braw)
        e.sign(out=b.sm("btmp", KC), in_=b.sm("braw", KC))
        e.activation(out=b.sm("bias1", KC), in_=b.sm("braw", KC),
                     func=AF.Abs).then_inc(b.sS.h, 1)
        b.sS.n += 1
    blk.scalar(scalar_sgn)
    s_sgn = b.sS.n

    def vector_bias2(e):
        e.wait_ge(b.sS.h, s_sgn)
        bias1 = b.sm("bias1", KC)
        braw = b.sm("braw", KC)
        e.tensor_scalar(out=braw, in0=bias1, scalar1=15.0, scalar2=None, op0=OP.mult)
        e.memset(bias1, 0.0)
        e.drain()
        for j in range(1, 16):
            e.tensor_scalar(out=b.sm("btmp2", KC), in0=braw, scalar1=j - 0.5,
                            scalar2=None, op0=OP.is_ge)
            e.drain()
            e.tensor_tensor(out=bias1, in0=bias1, in1=b.sm("btmp2", KC), op=OP.add)
            e.drain()
        e.tensor_scalar(out=bias1, in0=bias1, scalar1=1.0 / 15.0, scalar2=None, op0=OP.mult)
        e.drain()
        e.tensor_tensor(out=bias1, in0=bias1, in1=b.sm("btmp", KC), op=OP.mult)
        e.drain()
        e.tensor_scalar(out=bias1, in0=bias1, scalar1=b.sm("bsc", 1), scalar2=None,
                        op0=OP.mult).then_inc(b.sV.h, 1)
        b.sV.n += 1
    blk.vector(vector_bias2)
    b.v_pre = b.sV.n
    b.w_pre = w_done



def emit_phase1_ln0(b):
    blk = b.blk
    x_t = b.x_fm.rearrange("(k p) t -> k p t", p=128)

    w0 = b.sW.n
    s0 = b.sS.n
    t0 = b.sT.n

    def sync_a(e):
        for k in range(KC):
            if k >= 4:
                e.wait_ge(b.sT.h, t0 + (k - 4) + 1)
            b.dma(e, b.C4[:, (k % 4) * T:(k % 4 + 1) * T], x_t[k], b.sW)
    blk.sync(sync_a)

    def scalar_a(e):
        for k in range(KC):
            e.wait_ge(b.sW.h, w0 + 16 * (k + 1))
            if k >= 2:
                e.wait_ge(b.sT.h, t0 + (k - 2) + 1)
            e.activation(out=b.b_(k % 2), in_=b.C4[:, (k % 4) * T:(k % 4 + 1) * T],
                         func=AF.Square).then_inc(b.sS.h, 1)
            b.sS.n += 1
    blk.scalar(scalar_a)

    def tensor_a(e):
        for k in range(KC):
            e.wait_ge(b.sS.h, s0 + k + 1)
            ins = None
            for g in range(NG):
                e.matmul(b.PS[4 + g][0:1, :NT], b.onesc[:],
                         b.C4[:, (k % 4) * T + g * NT:(k % 4) * T + (g + 1) * NT],
                         start=(k == 0), stop=(k == KC - 1))
                ins = e.matmul(b.PS[6 + g][0:1, :NT], b.onesc[:],
                               b.b_(k % 2)[:, g * NT:(g + 1) * NT],
                               start=(k == 0), stop=(k == KC - 1))
            ins.then_inc(b.sT.h, 1)
            b.sT.n += 1
    blk.tensor(tensor_a)
    t_stats = b.sT.n

    def vector_stats(e):
        e.wait_ge(b.sT.h, t_stats)
        for g in range(NG):
            e.tensor_scalar(out=b.row1[:, g * NT:(g + 1) * NT], in0=b.PS[4 + g][0:1, :NT],
                            scalar1=1.0 / D, scalar2=None, op0=OP.mult)
            e.tensor_scalar(out=b.row2[:, g * NT:(g + 1) * NT], in0=b.PS[6 + g][0:1, :NT],
                            scalar1=1.0 / D, scalar2=None, op0=OP.mult)
        e.drain()
        vrow = b.M2[0:1, T:2 * T]
        e.tensor_tensor(out=vrow, in0=b.row1[:], in1=b.row1[:], op=OP.mult)
        e.drain()
        e.tensor_tensor(out=b.row2[:], in0=b.row2[:], in1=vrow, op=OP.subtract)
        e.drain()
        e.tensor_scalar(out=vrow, in0=b.row2[:], scalar1=EPS, scalar2=None, op0=OP.add)
        e.drain()
        e.reciprocal(out=b.row2[:], in_=vrow).then_inc(b.sV.h, 1)
        b.sV.n += 1
    blk.vector(vector_stats)
    v_rows = b.sV.n

    def scalar_rs(e):
        e.wait_ge(b.sV.h, v_rows)
        e.activation(out=b.row2[:], in_=b.row2[:], func=AF.Sqrt).then_inc(b.sS.h, 1)
        b.sS.n += 1
    blk.scalar(scalar_rs)
    s_rs = b.sS.n

    def vector_newton(e):
        e.wait_ge(b.sS.h, s_rs)
        rb = b.row1
        tmp = b.M2[0:1, 0:T]
        e.tensor_tensor(out=tmp, in0=b.row2[:], in1=b.row2[:], op=OP.mult)
        e.drain()
        e.tensor_tensor(out=tmp, in0=tmp, in1=b.M2[0:1, T:2 * T], op=OP.mult)
        e.drain()
        e.tensor_scalar(out=tmp, in0=tmp, scalar1=-0.5, scalar2=1.5,
                        op0=OP.mult, op1=OP.add)
        e.drain()
        e.tensor_tensor(out=b.row2[:], in0=b.row2[:], in1=tmp,
                        op=OP.mult).then_inc(b.sV.h, 1)
        b.sV.n += 1
    blk.vector(vector_newton)
    v_nt = b.sV.n

    def tensor_bc(e):
        e.wait_ge(b.sV.h, v_nt)
        ins = None
        for g in range(NG):
            e.matmul(b.PS[4 + g][:, :NT], b.ones1[:], b.row1[0:1, g * NT:(g + 1) * NT],
                     start=True, stop=True)
            ins = e.matmul(b.PS[6 + g][:, :NT], b.ones1[:], b.row2[0:1, g * NT:(g + 1) * NT],
                           start=True, stop=True)
        ins.then_inc(b.sT.h, 1)
        b.sT.n += 1
    blk.tensor(tensor_bc)
    t_bc = b.sT.n

    def vector_bc(e):
        e.wait_ge(b.sT.h, t_bc)
        for g in range(NG):
            e.tensor_copy(out=b.mb[:, g * NT:(g + 1) * NT], in_=b.PS[4 + g][:, :NT])
        ins = None
        for g in range(NG):
            ins = e.tensor_copy(out=b.rsb[:, g * NT:(g + 1) * NT], in_=b.PS[6 + g][:, :NT])
        ins.then_inc(b.sV.h, 1)
        b.sV.n += 1
    blk.vector(vector_bc)
    v_bc = b.sV.n

    w1 = b.sW.n

    def sync_b(e):
        for k in range(KC):
            e.wait_ge(b.sV.h, v_bc + max(0, k - 3))
            b.dma(e, b.C4[:, (k % 4) * T:(k % 4 + 1) * T], x_t[k], b.sW)
    blk.sync(sync_b)

    def vector_xn(e):
        e.wait_ge(b.sV.h, v_bc)
        for k in range(KC):
            e.wait_ge(b.sW.h, w1 + 16 * (k + 1))
            e.tensor_tensor(out=b.a_(k), in0=b.C4[:, (k % 4) * T:(k % 4 + 1) * T],
                            in1=b.mb, op=OP.subtract)
            e.tensor_tensor(out=b.a_(k), in0=b.a_(k), in1=b.rsb,
                            op=OP.mult).then_inc(b.sV.h, 1)
            b.sV.n += 1
    blk.vector(vector_xn)
    b.v_xn = b.sV.n

    def gp_xn(e):
        for k in range(KC):
            e.wait_ge(b.sV.h, v_bc + k + 1)
            b.dma(e, b.d_xn[k], b.a_(k), b.sG)
    blk.gpsimd(gp_xn)
    b.g_xn = b.sG.n

    w2 = b.sW.n

    def sync_gk(e):
        b.dma(e, b.W2[:, 0:KC * E], b.gk_s.rearrange("p k e -> p (k e)"), b.sW)
    blk.sync(sync_gk)

    t1 = b.sT.n
    v_gbase = b.sV.n

    def tensor_gate(e):
        e.wait_ge(b.sW.h, w2 + 16)
        e.wait_ge(b.sV.h, b.v_xn)
        gkv = b.W2[:, 0:KC * E].rearrange("p (k e) -> p k e", e=E)
        for tc in range(8):
            if tc >= 2:
                e.wait_ge(b.sV.h, v_gbase + 2 * (tc - 2) + 1)
            ins = None
            for k in range(KC):
                ins = e.matmul(b.PS[tc % 2][:, 0:E], b.a_(k)[:, tc * 128:(tc + 1) * 128],
                               gkv[:, k, :], start=(k == 0), stop=(k == KC - 1))
            ins.then_inc(b.sT.h, 1)
            b.sT.n += 1
    blk.tensor(tensor_gate)

    def vector_top(e):
        for tc in range(8):
            e.wait_ge(b.sT.h, t1 + tc + 1)
            base = 0 + tc * 32
            lg = b.C4[:, base:base + E]
            mx = b.C4[:, base + 8:base + 16]
            e.tensor_copy(out=lg, in_=b.PS[tc % 2][:, 0:E]).then_inc(b.sV.h, 1)
            b.sV.n += 1
            e.drain()
            e.max(out=mx, in_=lg)
            e.drain()
            e.engine_nop().then_inc(b.sV.h, 1)
            b.sV.n += 1
    blk.vector(vector_top)
    v_top = b.sV.n

    def scalar_w2(e):
        for tc in range(8):
            e.wait_ge(b.sV.h, v_top - 16 + 2 * tc + 2)
            base = 0 + tc * 32
            m1 = b.C4[:, base + 8:base + 9]
            m2 = b.C4[:, base + 9:base + 10]
            nm1 = b.C4[:, base + 12:base + 13]
            w2c = b.C4[:, base + 13:base + 14]
            e.mul(out=nm1, in_=m1, mul=-1.0)
            e.drain()
            e.activation(out=w2c, in_=m2, func=AF.Sigmoid, bias=nm1).then_inc(b.sS.h, 1)
            b.sS.n += 1
    blk.scalar(scalar_w2)
    s_w2 = b.sS.n

    def vector_cw(e):
        for tc in range(8):
            e.wait_ge(b.sS.h, s_w2 - 8 + tc + 1)
            base = 0 + tc * 32
            lg = b.C4[:, base:base + E]
            m1 = b.C4[:, base + 8:base + 9]
            m2 = b.C4[:, base + 9:base + 10]
            w2c = b.C4[:, base + 13:base + 14]
            w1c = b.C4[:, base + 14:base + 15]
            e.tensor_scalar(out=w1c, in0=w2c, scalar1=-1.0, scalar2=1.0,
                            op0=OP.mult, op1=OP.add)
            cw = b.C4[:, base + 16:base + 24]
            eq = b.C4[:, base + 24:base + 32]
            e.drain()
            e.tensor_scalar(out=eq, in0=lg, scalar1=m1, scalar2=None, op0=OP.is_equal)
            e.drain()
            e.tensor_scalar(out=cw, in0=eq, scalar1=w1c, scalar2=None, op0=OP.mult)
            e.tensor_scalar(out=eq, in0=lg, scalar1=m2, scalar2=None, op0=OP.is_equal)
            e.drain()
            e.scalar_tensor_tensor(out=cw, in0=eq, scalar=w2c, in1=cw,
                                   op0=OP.mult, op1=OP.add).then_inc(b.sV.h, 1)
            b.sV.n += 1
    blk.vector(vector_cw)
    v_cw = b.sV.n

    t2 = b.sT.n
    v_cwr_base = b.sV.n

    def tensor_cwT(e):
        for tc in range(8):
            e.wait_ge(b.sV.h, v_cw - 8 + tc + 1)
            if tc >= 2:
                e.wait_ge(b.sV.h, v_cwr_base + (tc - 2) + 1)
            base = 0 + tc * 32
            cw = b.C4[:, base + 16:base + 24]
            e.transpose(b.PS[2 + tc % 2][0:E, 0:128], cw,
                        b.sm("ident", 128)).then_inc(b.sT.h, 1)
            b.sT.n += 1
    blk.tensor(tensor_cwT)

    def vector_cwr(e):
        for tc in range(8):
            e.wait_ge(b.sT.h, t2 + tc + 1)
            e.tensor_copy(out=b.cwrow[:, tc * 128:(tc + 1) * 128],
                          in_=b.PS[2 + tc % 2][0:E, 0:128]).then_inc(b.sV.h, 1)
            b.sV.n += 1
    blk.vector(vector_cwr)
    b.v_cwrow = b.sV.n



def emit_phase2_attn(b, upto="all"):
    blk = b.blk
    xk = lambda k, g: b.a_(k)[:, g * NT:(g + 1) * NT]

    eb = b.sS.n
    t0 = emit_mm_dxd(b, lambda m: b.wf_s[m], xk, (b.sS, eb),
                     extra_tensor_waits=((b.sV, b.v_xn), (b.sV, b.v_cwrow)))

    def scalar_f(e):
        for mi in range(KC):
            for g in range(NG):
                e.wait_ge(b.sT.h, t0 + mi * NG + g + 1)
                psi = (mi % 2) * 2 + g
                e.activation(out=b.b_(mi)[:, g * NT:(g + 1) * NT],
                             in_=b.PS[psi][:, :NT], func=AF.Sigmoid).then_inc(b.sS.h, 1)
                b.sS.n += 1
    blk.scalar(scalar_f)

    eb2 = b.sS.n
    G0 = b.sG.n
    t1 = emit_mm_dxd(b, lambda m: b.wr_s[m], xk, (b.sS, eb2))

    def scalar_r(e):
        for mi in range(KC):
            if mi >= 4:
                e.wait_ge(b.sG.h, G0 + 16 * (mi - 3))
            for g in range(NG):
                e.wait_ge(b.sT.h, t1 + mi * NG + g + 1)
                psi = (mi % 2) * 2 + g
                e.activation(out=b.C4[:, (mi % 4) * T + g * NT:(mi % 4) * T + (g + 1) * NT],
                             in_=b.PS[psi][:, :NT], func=AF.Silu).then_inc(b.sS.h, 1)
                b.sS.n += 1
    blk.scalar(scalar_r)

    def gp_r(e):
        for mi in range(KC):
            e.wait_ge(b.sS.h, eb2 + (mi + 1) * NG)
            b.dma(e, b.d_r[mi], b.C4[:, (mi % 4) * T:(mi % 4 + 1) * T], b.sG)
    blk.gpsimd(gp_r)
    g_r = b.sG.n

    eb3 = b.sS.n
    GU0 = b.sG.n
    t2 = emit_mm_dxd(b, lambda m: b.wc_s[m], xk, (b.sS, eb3),
                     extra_tensor_waits=((b.sG, g_r),))

    def scalar_c(e):
        for mi in range(KC):
            if mi >= 2:
                e.wait_ge(b.sG.h, GU0 + 16 * (mi - 1))
            for g in range(NG):
                e.wait_ge(b.sT.h, t2 + mi * NG + g + 1)
                psi = (mi % 2) * 2 + g
                e.activation(out=b.C4[:, (mi % 2) * T + g * NT:(mi % 2) * T + (g + 1) * NT],
                             in_=b.PS[psi][:, :NT], func=AF.Silu).then_inc(b.sS.h, 1)
                b.sS.n += 1
    blk.scalar(scalar_c)

    def gp_c(e):
        for mi in range(KC):
            e.wait_ge(b.sS.h, eb3 + (mi + 1) * NG)
            b.dma(e, b.d_u[mi], b.C4[:, (mi % 2) * T:(mi % 2 + 1) * T], b.sG)
    blk.gpsimd(gp_c)
    g_c = b.sG.n

    g2 = b.sG.n

    def gp_c_back(e):
        e.wait_ge(b.sG.h, g_c)
        e.wait_ge(b.sT.h, t2 + KC * NG)
        for mi in range(KC):
            b.dma(e, b.a_(mi), b.d_u[mi], b.sG)
    blk.gpsimd(gp_c_back)
    g_cb = b.sG.n

    LgT = 10

    def tview(buf, lvl, j0, stride_nodes, count):
        off = j0 * (1 << lvl) + (1 << lvl) - 1
        return buf[:, :].rearrange("p (k t) -> p k t", k=KC)[
            :, :, off::stride_nodes * (1 << lvl)][:, :, :count]

    v_pre_tree = b.sV.n

    def vector_up(e):
        e.wait_ge(b.sG.h, g_cb)
        for lvl in range(LgT):
            half = (T >> lvl) // 2
            step = 1 << (lvl + 1)
            if step > 512:
                for k in range(KC):
                    oL = k * T + (1 << lvl) - 1
                    oR = k * T + step - 1
                    hL, hR = b.A[:, oL:oL + 1], b.A[:, oR:oR + 1]
                    fL, fR = b.B[:, oL:oL + 1], b.B[:, oR:oR + 1]
                    tmp = b.C4[:, k:k + 1]
                    e.tensor_tensor(out=tmp, in0=hL, in1=hR, op=OP.subtract)
                    e.drain()
                    e.tensor_tensor(out=tmp, in0=fR, in1=tmp, op=OP.mult)
                    e.drain()
                    e.tensor_tensor(out=hR, in0=hR, in1=tmp, op=OP.add)
                    e.tensor_tensor(out=fR, in0=fL, in1=fR, op=OP.mult)
                e.drain()
                continue
            tot = KC * half
            nb = 2 if tot > 4096 else 1
            for bi in range(nb):
                cpb = tot // nb
                def vv(buf, j0):
                    off = j0 * (1 << lvl) + (1 << lvl) - 1
                    return buf[:, off::step][:, bi * cpb:(bi + 1) * cpb]
                hL, hR = vv(b.A, 0), vv(b.A, 1)
                fL, fR = vv(b.B, 0), vv(b.B, 1)
                tmp = b.C4[:, 0:cpb]
                e.tensor_tensor(out=tmp, in0=hL, in1=hR, op=OP.subtract)
                e.drain()
                e.tensor_tensor(out=tmp, in0=fR, in1=tmp, op=OP.mult)
                e.drain()
                e.tensor_tensor(out=hR, in0=hR, in1=tmp, op=OP.add)
                e.tensor_tensor(out=fR, in0=fL, in1=fR, op=OP.mult)
            e.drain()
        bnd = b.sm("bnd", KC)
        e.tensor_copy(out=bnd, in_=b.A[:, T - 1::T]).then_inc(b.sV.h, 1)
        b.sV.n += 1
    blk.vector(vector_up)
    v_up = b.sV.n

    cc0 = b.sC.n
    gb0 = b.sG.n

    def gp_cc1(e):
        e.wait_ge(b.sV.h, v_up)
        b.dma(e, b.cc1_src, b.sm("bnd", KC), b.sG)
        e.wait_ge(b.sG.h, gb0 + 16)
        e.collective_compute(
            "AllGather", OP.bypass, replica_groups=[list(range(8))],
            ins=[b.cc1_src.opt()], outs=[b.cc1_dst.opt()],
        ).then_inc(b.sC.h, 1)
        b.sC.n += 1
        e.wait_ge(b.sC.h, cc0 + 1)
        for r in range(8):
            b.dma(e, b.C4[:, 4 * T + r * KC:4 * T + (r + 1) * KC], b.cc1_dst[r], b.sG)
    blk.gpsimd(gp_cc1)
    g_cc1 = b.sG.n

    def vector_down(e):
        e.wait_ge(b.sG.h, g_cc1)
        hseed = b.sm("hin2", KC)
        npc = b.sm("hin1", KC)[:, 0:1]
        e.memset(hseed, 0.0)
        e.drain()
        for r in range(8):
            e.scalar_tensor_tensor(out=hseed,
                                   in0=b.C4[:, 4 * T + r * KC:4 * T + (r + 1) * KC],
                                   scalar=b.sm("sel", 8)[:, r:r + 1], in1=hseed,
                                   op0=OP.mult, op1=OP.add)
            e.drain()
        e.tensor_scalar(out=npc, in0=b.sm("par"), scalar1=-1.0, scalar2=1.0,
                        op0=OP.mult, op1=OP.add)
        e.drain()
        A3 = b.A[:, :].rearrange("p (k t) -> p k t", k=KC)
        B3 = b.B[:, :].rearrange("p (k t) -> p k t", k=KC)
        ctmp = b.sm("c0", KC)
        for lvl in range(LgT, -1, -1):
            p0 = (1 << lvl) - 1
            h0v = b.A[:, p0::T]
            f0v = b.B[:, p0::T]
            e.tensor_tensor(out=ctmp, in0=hseed, in1=h0v, op=OP.subtract)
            e.drain()
            e.tensor_tensor(out=ctmp, in0=ctmp, in1=f0v, op=OP.mult)
            e.drain()
            e.tensor_scalar(out=ctmp, in0=ctmp, scalar1=npc, scalar2=None,
                            op0=OP.mult)
            e.drain()
            for k in range(KC):
                o = k * T + p0
                e.tensor_tensor(out=b.A[:, o:o + 1], in0=b.A[:, o:o + 1],
                                in1=ctmp[:, k:k + 1], op=OP.add)
            e.drain()
            if lvl == 0:
                break
            m = T >> lvl
            cnt = m - 1
            if cnt <= 0:
                continue
            lvm = lvl - 1
            if cnt == 1:
                hS = b.A[:, (1 << lvl) - 1::T]
                hX = b.A[:, 3 * (1 << lvm) - 1::T]
                fX = b.B[:, 3 * (1 << lvm) - 1::T]
                tmp = b.C4[:, 0:KC]
                e.tensor_tensor(out=tmp, in0=hS, in1=hX, op=OP.subtract)
                e.drain()
                e.tensor_tensor(out=tmp, in0=fX, in1=tmp, op=OP.mult)
                e.drain()
                for k in range(KC):
                    o = k * T + 3 * (1 << lvm) - 1
                    e.tensor_tensor(out=b.A[:, o:o + 1], in0=b.A[:, o:o + 1],
                                    in1=tmp[:, k:k + 1], op=OP.add)
                e.drain()
            else:
                hS = A3[:, :, (1 << lvl) - 1::(1 << lvl)][:, :, :cnt]
                hX = A3[:, :, 3 * (1 << lvm) - 1::(1 << lvl)][:, :, :cnt]
                fX = B3[:, :, 3 * (1 << lvm) - 1::(1 << lvl)][:, :, :cnt]
                nb = 2 if cnt * KC > 4096 else 1
                for bi in range(nb):
                    ksl = slice(bi * (KC // nb), (bi + 1) * (KC // nb))
                    tmp = b.C4[:, 0:(KC // nb) * cnt].rearrange(
                        "p (k t) -> p k t", k=KC // nb)
                    e.tensor_tensor(out=tmp, in0=hS[:, ksl], in1=hX[:, ksl], op=OP.subtract)
                    e.drain()
                    e.tensor_tensor(out=tmp, in0=fX[:, ksl], in1=tmp, op=OP.mult)
                    e.drain()
                    e.tensor_tensor(out=hX[:, ksl], in0=hX[:, ksl], in1=tmp, op=OP.add)
                    e.drain()
        e.engine_nop().then_inc(b.sV.h, 1)
        b.sV.n += 1
    blk.vector(vector_down)
    v_h = b.sV.n
    v_hin2 = v_h
    if upto == "h":
        return

    s1 = b.sS.n

    t3_base = b.sT.n

    def scalar_sq(e):
        e.wait_ge(b.sV.h, v_h)
        for mi in range(KC):
            if mi >= 2:
                e.wait_ge(b.sT.h, t3_base + (mi - 2) + 1)
            e.activation(out=b.C4[:, 2 * T + (mi % 2) * T:2 * T + (mi % 2 + 1) * T],
                         in_=b.a_(mi), func=AF.Square).then_inc(b.sS.h, 1)
            b.sS.n += 1
    blk.scalar(scalar_sq)

    t3 = b.sT.n

    def tensor_ss(e):
        for mi in range(KC):
            e.wait_ge(b.sS.h, s1 + mi + 1)
            ins = None
            for g in range(NG):
                ins = e.matmul(b.PS[4 + g][0:1, :NT], b.onesc[:],
                               b.C4[:, 2 * T + (mi % 2) * T + g * NT:2 * T + (mi % 2) * T + (g + 1) * NT],
                               start=(mi == 0), stop=(mi == KC - 1))
            ins.then_inc(b.sT.h, 1)
            b.sT.n += 1
    blk.tensor(tensor_ss)
    t_ss = b.sT.n

    def vector_rs(e):
        e.wait_ge(b.sT.h, t_ss)
        for g in range(NG):
            e.tensor_scalar(out=b.row2[:, g * NT:(g + 1) * NT], in0=b.PS[4 + g][0:1, :NT],
                            scalar1=1.0 / D, scalar2=EPS, op0=OP.mult, op1=OP.add)
        e.drain()
        e.reciprocal(out=b.row1[:], in_=b.row2[:]).then_inc(b.sV.h, 1)
        b.sV.n += 1
    blk.vector(vector_rs)
    v_rs1 = b.sV.n

    def scalar_rs1(e):
        e.wait_ge(b.sV.h, v_rs1)
        e.activation(out=b.row1[:], in_=b.row1[:], func=AF.Sqrt).then_inc(b.sS.h, 1)
        b.sS.n += 1
    blk.scalar(scalar_rs1)
    s_rs1 = b.sS.n

    def vector_nt1(e):
        e.wait_ge(b.sS.h, s_rs1)
        tmp = b.M2[0:1, 0:T]
        e.tensor_tensor(out=tmp, in0=b.row1[:], in1=b.row1[:], op=OP.mult)
        e.drain()
        e.tensor_tensor(out=tmp, in0=tmp, in1=b.row2[:], op=OP.mult)
        e.drain()
        e.tensor_scalar(out=tmp, in0=tmp, scalar1=-0.5, scalar2=1.5,
                        op0=OP.mult, op1=OP.add)
        e.drain()
        e.tensor_tensor(out=b.row1[:], in0=b.row1[:], in1=tmp,
                        op=OP.mult).then_inc(b.sV.h, 1)
        b.sV.n += 1
    blk.vector(vector_nt1)
    v_nt1 = b.sV.n

    t4 = b.sT.n

    def tensor_rsb(e):
        e.wait_ge(b.sV.h, v_nt1)
        ins = None
        for g in range(NG):
            ins = e.matmul(b.PS[4 + g][:, :NT], b.ones1[:],
                           b.row1[0:1, g * NT:(g + 1) * NT], start=True, stop=True)
        ins.then_inc(b.sT.h, 1)
        b.sT.n += 1
    blk.tensor(tensor_rsb)
    t_rsb = b.sT.n

    def vector_rsb(e):
        e.wait_ge(b.sT.h, t_rsb)
        ins = None
        for g in range(NG):
            ins = e.tensor_copy(out=b.rsb[:, g * NT:(g + 1) * NT], in_=b.PS[4 + g][:, :NT])
        ins.then_inc(b.sV.h, 1)
        b.sV.n += 1
    blk.vector(vector_rsb)
    v_rsb = b.sV.n

    g3 = b.sG.n

    def gp_r_back(e):
        e.wait_ge(b.sV.h, v_rsb)
        e.wait_ge(b.sG.h, g3)
        for mi in range(KC):
            if mi >= 2:
                e.wait_ge(b.sV.h, v_rsb + (mi - 2) + 1)
            b.dma(e, b.C4[:, (mi % 2) * T:(mi % 2 + 1) * T], b.d_r[mi], b.sG)
    blk.gpsimd(gp_r_back)

    def vector_rq(e):
        for mi in range(KC):
            e.wait_ge(b.sG.h, g3 + 16 * (mi + 1))
            e.scalar_tensor_tensor(out=b.b_(mi), in0=b.a_(mi),
                                   scalar=b.smc("lnscale", mi), in1=b.rsb,
                                   op0=OP.mult, op1=OP.mult)
            e.tensor_tensor(out=b.b_(mi), in0=b.b_(mi),
                            in1=b.C4[:, (mi % 2) * T:(mi % 2 + 1) * T],
                            op=OP.mult).then_inc(b.sV.h, 1)
            b.sV.n += 1
    blk.vector(vector_rq)
    v_rq = b.sV.n

    eb4 = b.sV.n
    G4 = b.sG.n
    rqk = lambda k, g: b.b_(k)[:, g * NT:(g + 1) * NT]
    t5 = emit_mm_dxd(b, lambda m: b.wo_s[m], rqk, (b.sV, eb4),
                     extra_tensor_waits=((b.sV, v_rq),))

    def vector_s(e):
        for mi in range(KC):
            if mi >= 2:
                e.wait_ge(b.sG.h, G4 + 16 * (mi - 1))
            for g in range(NG):
                e.wait_ge(b.sT.h, t5 + mi * NG + g + 1)
                psi = (mi % 2) * 2 + g
                e.tensor_copy(out=b.C4[:, (mi % 2) * T + g * NT:(mi % 2) * T + (g + 1) * NT],
                              in_=b.PS[psi][:, :NT]).then_inc(b.sV.h, 1)
                b.sV.n += 1
    blk.vector(vector_s)

    def gp_s(e):
        for mi in range(KC):
            e.wait_ge(b.sV.h, eb4 + (mi + 1) * NG)
            b.dma(e, b.d_s[mi], b.C4[:, (mi % 2) * T:(mi % 2 + 1) * T], b.sG)
    blk.gpsimd(gp_s)
    b.g_s_done = b.sG.n
    b.v_rq = v_rq
    b.t_wo_end = t5 + KC * NG



def emit_debug_out(b, src_dram, nk=KC):
    blk = b.blk

    def gp(e):
        e.wait_ge(b.sV.h, b.sV.n)
        e.wait_ge(b.sS.h, b.sS.n)
        e.wait_ge(b.sT.h, b.sT.n)
        e.wait_ge(b.sG.h, b.sG.n)
        out_t = b.out.rearrange("(k p) t -> k p t", p=128)
        for k in range(nk):
            b.dma(e, b.C4[:, 0:T], src_dram[k], b.sG)
            e.wait_ge(b.sG.h, b.sG.n)
            b.dma(e, out_t[k], b.C4[:, 0:T], b.sG)
            e.wait_ge(b.sG.h, b.sG.n)
    blk.gpsimd(gp)


def emit_debug_big(b, src_dram, off):
    blk = b.blk

    def gp(e):
        e.wait_ge(b.sV.h, b.sV.n)
        e.wait_ge(b.sS.h, b.sS.n)
        e.wait_ge(b.sT.h, b.sT.n)
        e.wait_ge(b.sG.h, b.sG.n)
        out_t = b.out.rearrange("(k p) t -> k p t", p=128)
        nk = min(KC, src_dram.shape[0] - off)
        for k in range(nk):
            b.dma(e, b.C4[:, 0:T], src_dram[off + k], b.sG)
            e.wait_ge(b.sG.h, b.sG.n)
            b.dma(e, out_t[k], b.C4[:, 0:T], b.sG)
            e.wait_ge(b.sG.h, b.sG.n)
    blk.gpsimd(gp)


def finish(b):
    blk = b.blk

    def gp(e):
        e.wait_ge(b.sG.h, b.sG.n)
        e.wait_ge(b.sW.h, b.sW.n)
    blk.gpsimd(gp)
    b.close()



def emit_phase3_moe(b):
    blk = b.blk
    def gp_xnb(e):
        e.wait_ge(b.sV.h, b.v_rq)
        e.wait_ge(b.sG.h, b.g_xn)
        for k in range(KC):
            b.dma(e, b.a_(k), b.d_xn[k], b.sG)
    blk.gpsimd(gp_xnb)
    g_xnb = b.sG.n

    dg_count = {}
    v_prev_cwb = None
    t_prev_cwb = None
    for ei in range(E):
        grow0 = b.sG.n

        def gp_cwrow(e, ei=ei, tp=t_prev_cwb):
            if ei == 0:
                e.wait_ge(b.sV.h, b.v_cwrow)
            if tp is not None:
                e.wait_ge(b.sT.h, tp)
            e.dma_start(out=b.row1[:], in_=b.cwrow[ei:ei + 1, :],
                        allow_slow_non_contiguous=True).then_inc(b.sG.h, 16)
            b.sG.n += 16
        blk.gpsimd(gp_cwrow)
        g_cwrow = b.sG.n

        def tensor_cwb(e, ei=ei, vp=v_prev_cwb, g_cwrow=g_cwrow):
            if ei == 0:
                e.wait_ge(b.sG.h, g_xnb)
                e.wait_ge(b.sT.h, b.t_wo_end)
            e.wait_ge(b.sG.h, g_cwrow)
            if vp is not None:
                e.wait_ge(b.sV.h, vp)
            ins = None
            for g in range(NG):
                ins = e.matmul(b.PS[6 + g][:, :NT], b.ones1[:],
                               b.row1[0:1, g * NT:(g + 1) * NT],
                               start=True, stop=True)
            ins.then_inc(b.sT.h, 1)
            b.sT.n += 1
        blk.tensor(tensor_cwb)
        t_cwb = b.sT.n
        t_prev_cwb = t_cwb

        def vector_cwb(e, t_cwb=t_cwb):
            e.wait_ge(b.sT.h, t_cwb)
            ins = None
            for g in range(NG):
                ins = e.tensor_copy(out=b.M2[:, g * NT:(g + 1) * NT],
                                    in_=b.PS[6 + g][:, :NT])
            ins.then_inc(b.sV.h, 1)
            b.sV.n += 1
        blk.vector(vector_cwb)
        v_cwb = b.sV.n
        v_prev_cwb = v_cwb

        w0 = b.sW.n
        th0 = b.sT.n
        ss0 = b.sS.n
        vv0 = b.sV.n

        def sync_h(e, ei=ei, w0=w0, th0=th0):
            for m in range(KC):
                e.wait_ge(b.sT.h, th0 + max(0, (m - 1) * 4))
                b.dma(e, b.W2[:, (m % 2) * D:(m % 2) * D + D], b.wi0_s[ei][m], b.sW)
                b.dma(e, b.W4[:, (m % 2) * D:(m % 2) * D + D], b.wi1_s[ei][m], b.sW)
        blk.sync(sync_h)

        def tensor_h(e, w0=w0, ss0=ss0, vv0=vv0, v_cwb=v_cwb):
            e.wait_ge(b.sV.h, v_cwb)
            for m in range(KC):
                e.wait_ge(b.sW.h, w0 + 32 * (m + 1))
                for g in range(NG):
                    if m >= 1:
                        e.wait_ge(b.sS.h, ss0 + (m - 1) * NG + g + 1)
                    ins = None
                    for k in range(KC):
                        ins = e.matmul(b.PS[g][:, :NT],
                                       b.W2[:, (m % 2) * D + k * 128:(m % 2) * D + k * 128 + 128],
                                       b.a_(k)[:, g * NT:(g + 1) * NT],
                                       start=(k == 0), stop=(k == KC - 1))
                    ins.then_inc(b.sT.h, 1)
                    b.sT.n += 1
                for g in range(NG):
                    if m >= 1:
                        e.wait_ge(b.sV.h, vv0 + (m - 1) * 4 + g + 1)
                    ins = None
                    for k in range(KC):
                        ins = e.matmul(b.PS[2 + g][:, :NT],
                                       b.W4[:, (m % 2) * D + k * 128:(m % 2) * D + k * 128 + 128],
                                       b.a_(k)[:, g * NT:(g + 1) * NT],
                                       start=(k == 0), stop=(k == KC - 1))
                    ins.then_inc(b.sT.h, 1)
                    b.sT.n += 1
        blk.tensor(tensor_h)
        th_end = b.sT.n

        def scalar_h(e, th0=th0, vv0=vv0):
            for m in range(KC):
                for g in range(NG):
                    e.wait_ge(b.sT.h, th0 + m * 4 + g + 1)
                    if m >= 1:
                        e.wait_ge(b.sV.h, vv0 + (m - 1) * 2 * NG + NG + g + 1)
                    e.activation(out=b.C4[:, g * NT:(g + 1) * NT],
                                 in_=b.PS[g][:, :NT], func=AF.Silu).then_inc(b.sS.h, 1)
                    b.sS.n += 1
        blk.scalar(scalar_h)

        def vector_h(e, th0=th0, ss0=ss0):
            for m in range(KC):
                for g in range(NG):
                    e.wait_ge(b.sT.h, th0 + m * 4 + 2 + g + 1)
                    e.tensor_tensor(out=b.C4[:, T + g * NT:T + (g + 1) * NT],
                                    in0=b.PS[2 + g][:, :NT],
                                    in1=b.M2[:, g * NT:(g + 1) * NT],
                                    op=OP.mult).then_inc(b.sV.h, 1)
                    b.sV.n += 1
                for g in range(NG):
                    e.wait_ge(b.sS.h, ss0 + m * NG + g + 1)
                    e.tensor_tensor(out=b.b_(m)[:, g * NT:(g + 1) * NT],
                                    in0=b.C4[:, g * NT:(g + 1) * NT],
                                    in1=b.C4[:, T + g * NT:T + (g + 1) * NT],
                                    op=OP.mult).then_inc(b.sV.h, 1)
                    b.sV.n += 1
        blk.vector(vector_h)
        v_h_end = b.sV.n

        eb = b.sV.n
        G0 = b.sG.n
        t_o = emit_mm_dxd(b, lambda m, ei=ei: b.wom_s[ei][m],
                          lambda k, g: b.b_(k)[:, g * NT:(g + 1) * NT],
                          (b.sV, eb), extra_tensor_waits=((b.sV, v_h_end),))

        def vector_o(e, t_o=t_o, G0=G0):
            for mi in range(KC):
                for g in range(NG):
                    e.wait_ge(b.sT.h, t_o + mi * NG + g + 1)
                    idx = mi * NG + g
                    if idx >= 4:
                        e.wait_ge(b.sG.h, G0 + 16 * (idx - 3))
                    slot = 2 * T + (idx % 4) * NT
                    e.tensor_copy(out=b.C4[:, slot:slot + NT],
                                  in_=b.PS[(mi % 2) * 2 + g][:, :NT]).then_inc(b.sV.h, 1)
                    b.sV.n += 1
        blk.vector(vector_o)

        def gp_o(e, ei=ei, eb=eb):
            for mi in range(KC):
                for g in range(NG):
                    idx = mi * NG + g
                    e.wait_ge(b.sV.h, eb + idx + 1)
                    if ei > 0:
                        e.wait_ge(b.sG.h, dg_count[(mi, g)])
                    slot = 2 * T + (idx % 4) * NT
                    dst = b.d_g[mi][:, g * NT:(g + 1) * NT]
                    if ei == 0:
                        e.dma_start(out=dst, in_=b.C4[:, slot:slot + NT]).then_inc(b.sG.h, 16)
                    else:
                        e.dma_start(out=dst, in_=b.C4[:, slot:slot + NT],
                                    accum_op=OP.add).then_inc(b.sG.h, 16)
                    b.sG.n += 16
                    dg_count[(mi, g)] = b.sG.n
        blk.gpsimd(gp_o)
    b.g_moe_end = b.sG.n
    b.t_moe_end = b.sT.n


def emit_phase4_update(b):
    blk = b.blk
    x_t = b.x_fm.rearrange("(k p) t -> k p t", p=128)
    w0 = b.sW.n
    g0 = b.sG.n
    s0 = b.sS.n
    v0 = b.sV.n
    t0 = b.sT.n

    def sync_p4(e):
        e.wait_ge(b.sT.h, b.t_moe_end)
        for k in range(KC):
            if k >= 2:
                e.wait_ge(b.sV.h, v0 + (k - 2) * 2 + 2)
            b.dma(e, b.W2[:, (k % 2) * T:(k % 2 + 1) * T], x_t[k], b.sW)
            b.dma(e, b.W4[:, (k % 2) * T:(k % 2 + 1) * T], b.d_s[k], b.sW)
    blk.sync(sync_p4)

    def gp_p4(e):
        e.wait_ge(b.sG.h, b.g_moe_end)
        for k in range(KC):
            if k >= 2:
                e.wait_ge(b.sS.h, s0 + (k - 2) * 2 + 1)
            b.dma(e, b.C4[:, (k % 2) * T:(k % 2 + 1) * T], b.d_g[k], b.sG)
    blk.gpsimd(gp_p4)

    def scalar_p4(e):
        for k in range(KC):
            e.wait_ge(b.sG.h, g0 + 16 * (k + 1))
            if k >= 2:
                e.wait_ge(b.sV.h, v0 + (k - 2) * 2 + 1)
            e.activation(out=b.C4[:, 2 * T + (k % 2) * T:2 * T + (k % 2 + 1) * T],
                         in_=b.C4[:, (k % 2) * T:(k % 2 + 1) * T],
                         func=AF.Silu).then_inc(b.sS.h, 1)
            b.sS.n += 1
            e.wait_ge(b.sV.h, v0 + k * 2 + 2)
            if k >= 2:
                e.wait_ge(b.sT.h, t0 + (k - 2) + 1)
            e.activation(out=b.b_(k % 2), in_=b.a_(k),
                         func=AF.Square).then_inc(b.sS.h, 1)
            b.sS.n += 1
    blk.scalar(scalar_p4)

    def vector_p4(e):
        for k in range(KC):
            e.wait_ge(b.sW.h, w0 + 32 * (k + 1))
            e.wait_ge(b.sS.h, s0 + k * 2 + 1)
            e.tensor_tensor(out=b.W4[:, (k % 2) * T:(k % 2 + 1) * T],
                            in0=b.W4[:, (k % 2) * T:(k % 2 + 1) * T],
                            in1=b.C4[:, 2 * T + (k % 2) * T:2 * T + (k % 2 + 1) * T],
                            op=OP.mult).then_inc(b.sV.h, 1)
            b.sV.n += 1
            e.tensor_tensor(out=b.a_(k), in0=b.W2[:, (k % 2) * T:(k % 2 + 1) * T],
                            in1=b.W4[:, (k % 2) * T:(k % 2 + 1) * T],
                            op=OP.add).then_inc(b.sV.h, 1)
            b.sV.n += 1
    blk.vector(vector_p4)
    v_x1 = b.sV.n

    def tensor_p4(e):
        for k in range(KC):
            e.wait_ge(b.sS.h, s0 + k * 2 + 2)
            ins = None
            for g in range(NG):
                e.matmul(b.PS[4 + g][0:1, :NT], b.onesc[:],
                         b.a_(k)[:, g * NT:(g + 1) * NT],
                         start=(k == 0), stop=(k == KC - 1))
                ins = e.matmul(b.PS[6 + g][0:1, :NT], b.onesc[:],
                               b.b_(k % 2)[:, g * NT:(g + 1) * NT],
                               start=(k == 0), stop=(k == KC - 1))
            ins.then_inc(b.sT.h, 1)
            b.sT.n += 1
    blk.tensor(tensor_p4)
    t_stats = b.sT.n

    def gp_x1(e):
        for k in range(KC):
            e.wait_ge(b.sV.h, v0 + 2 * k + 2)
            b.dma(e, b.d_x1[k], b.a_(k), b.sG)
    blk.gpsimd(gp_x1)
    b.g_x1 = b.sG.n

    def vector_st(e):
        e.wait_ge(b.sT.h, t_stats)
        for g in range(NG):
            e.tensor_scalar(out=b.row1[:, g * NT:(g + 1) * NT], in0=b.PS[4 + g][0:1, :NT],
                            scalar1=1.0 / D, scalar2=None, op0=OP.mult)
            e.tensor_scalar(out=b.row2[:, g * NT:(g + 1) * NT], in0=b.PS[6 + g][0:1, :NT],
                            scalar1=1.0 / D, scalar2=None, op0=OP.mult)
        e.drain()
        vrow = b.M2[0:1, T:2 * T]
        e.tensor_tensor(out=vrow, in0=b.row1[:], in1=b.row1[:], op=OP.mult)
        e.drain()
        e.tensor_tensor(out=b.row2[:], in0=b.row2[:], in1=vrow, op=OP.subtract)
        e.drain()
        e.tensor_scalar(out=vrow, in0=b.row2[:], scalar1=EPS, scalar2=None, op0=OP.add)
        e.drain()
        e.reciprocal(out=b.row2[:], in_=vrow).then_inc(b.sV.h, 1)
        b.sV.n += 1
    blk.vector(vector_st)
    v_rows = b.sV.n

    def scalar_rt(e):
        e.wait_ge(b.sV.h, v_rows)
        e.activation(out=b.row2[:], in_=b.row2[:], func=AF.Sqrt).then_inc(b.sS.h, 1)
        b.sS.n += 1
    blk.scalar(scalar_rt)
    s_rt = b.sS.n

    def vector_nt(e):
        e.wait_ge(b.sS.h, s_rt)
        tmp = b.M2[0:1, 0:T]
        e.tensor_tensor(out=tmp, in0=b.row2[:], in1=b.row2[:], op=OP.mult)
        e.drain()
        e.tensor_tensor(out=tmp, in0=tmp, in1=b.M2[0:1, T:2 * T], op=OP.mult)
        e.drain()
        e.tensor_scalar(out=tmp, in0=tmp, scalar1=-0.5, scalar2=1.5,
                        op0=OP.mult, op1=OP.add)
        e.drain()
        e.tensor_tensor(out=b.row2[:], in0=b.row2[:], in1=tmp,
                        op=OP.mult).then_inc(b.sV.h, 1)
        b.sV.n += 1
    blk.vector(vector_nt)
    v_nt = b.sV.n

    def tensor_bc(e):
        e.wait_ge(b.sV.h, v_nt)
        ins = None
        for g in range(NG):
            e.matmul(b.PS[4 + g][:, :NT], b.ones1[:], b.row1[0:1, g * NT:(g + 1) * NT],
                     start=True, stop=True)
            ins = e.matmul(b.PS[6 + g][:, :NT], b.ones1[:], b.row2[0:1, g * NT:(g + 1) * NT],
                           start=True, stop=True)
        ins.then_inc(b.sT.h, 1)
        b.sT.n += 1
    blk.tensor(tensor_bc)
    t_bc = b.sT.n

    def vector_bc(e):
        e.wait_ge(b.sT.h, t_bc)
        for g in range(NG):
            e.tensor_copy(out=b.mb[:, g * NT:(g + 1) * NT], in_=b.PS[4 + g][:, :NT])
        ins = None
        for g in range(NG):
            ins = e.tensor_copy(out=b.rsb[:, g * NT:(g + 1) * NT], in_=b.PS[6 + g][:, :NT])
        ins.then_inc(b.sV.h, 1)
        b.sV.n += 1
    blk.vector(vector_bc)
    v_bc = b.sV.n

    def vector_xn1(e):
        e.wait_ge(b.sV.h, v_bc)
        for k in range(KC):
            e.tensor_tensor(out=b.b_(k), in0=b.a_(k), in1=b.mb, op=OP.subtract)
            e.tensor_tensor(out=b.b_(k), in0=b.b_(k), in1=b.rsb,
                            op=OP.mult).then_inc(b.sV.h, 1)
            b.sV.n += 1
    blk.vector(vector_xn1)
    b.v_xn1 = b.sV.n


def emit_phase5_proj(b):
    blk = b.blk
    eb = b.sV.n
    G0 = b.sG.n
    t0 = emit_mm_dxd(b, lambda m: b.win_s[m],
                     lambda k, g: b.b_(k)[:, g * NT:(g + 1) * NT],
                     (b.sV, eb), mo_list=list(range(67)),
                     extra_tensor_waits=((b.sV, b.v_xn1),))

    def vector_ev(e):
        for mi in range(67):
            for g in range(NG):
                e.wait_ge(b.sT.h, t0 + mi * NG + g + 1)
                idx = mi * NG + g
                if idx >= 4:
                    e.wait_ge(b.sG.h, G0 + 16 * (idx - 3))
                slot = (idx % 4) * T
                slot = (idx % 4) * NT
                if mi == 66:
                    e.tensor_copy(out=b.sm2("dt", T)[:, g * NT:(g + 1) * NT],
                                  in_=b.PS[(mi % 2) * 2 + g][0:NH, :NT]).then_inc(b.sV.h, 1)
                    b.sV.n += 1
                    continue
                e.tensor_copy(out=b.C4[:, slot:slot + NT],
                              in_=b.PS[(mi % 2) * 2 + g][:, :NT]).then_inc(b.sV.h, 1)
                b.sV.n += 1
                if 32 <= mi < 66 and g == NG - 1:
                    hbase = b.sm_off["halo"] + (mi - 32) * 3
                    e.drain()
                    e.tensor_copy(out=b.SM[:, hbase:hbase + 3],
                                  in_=b.C4[:, slot + NT - 3:slot + NT]).then_inc(b.sV.h, 1)
                    b.sV.n += 1
    blk.vector(vector_ev)

    def gp_route(e):
        vcount = eb
        for mi in range(67):
            for g in range(NG):
                vcount += 1
                if mi == 66:
                    continue
                if 32 <= mi < 66 and g == NG - 1:
                    vcount += 1
                e.wait_ge(b.sV.h, vcount)
                idx = mi * NG + g
                slot = (idx % 4) * NT
                if mi < 32:
                    dst = b.d_z[mi][:, g * NT:(g + 1) * NT]
                else:
                    dst = b.d_xbc[mi - 32][:, g * NT:(g + 1) * NT]
                b.dma(e, dst, b.C4[:, slot:slot + NT], b.sG)
    blk.gpsimd(gp_route)
    b.g_proj = b.sG.n
    b.v_proj = b.sV.n

    cc0 = b.sC.n
    gb0 = b.sG.n

    def gp_cc2(e):
        e.wait_ge(b.sV.h, b.v_proj)
        b.dma(e, b.cc2_src, b.sm("halo", 102), b.sG)
        e.wait_ge(b.sG.h, gb0 + 16)
        e.collective_compute(
            "AllGather", OP.bypass, replica_groups=[list(range(8))],
            ins=[b.cc2_src.opt()], outs=[b.cc2_dst.opt()],
        ).then_inc(b.sC.h, 1)
        b.sC.n += 1
        e.wait_ge(b.sC.h, cc0 + 1)
        for r in range(8):
            b.dma(e, b.C4[:, 2 * T + r * 102:2 * T + (r + 1) * 102], b.cc2_dst[r], b.sG)
    blk.gpsimd(gp_cc2)
    g_cc2 = b.sG.n

    def vector_halo(e):
        e.wait_ge(b.sG.h, g_cc2)
        hrx = b.sm("halo_rx", 102)
        e.memset(hrx, 0.0)
        e.drain()
        for r in range(8):
            e.scalar_tensor_tensor(out=hrx,
                                   in0=b.C4[:, 2 * T + r * 102:2 * T + (r + 1) * 102],
                                   scalar=b.sm("sel", 8)[:, r:r + 1], in1=hrx,
                                   op0=OP.mult, op1=OP.add)
            e.drain()
        e.engine_nop().then_inc(b.sV.h, 1)
        b.sV.n += 1
    blk.vector(vector_halo)
    b.v_halo = b.sV.n


def emit_phase5b_ffn1(b):
    blk = b.blk
    for oh in range(2):
        w0 = b.sW.n
        s0 = b.sS.n
        v0 = b.sV.n

        def sync_pn(e, oh=oh, w0=w0, v0=v0, s0=s0):
            if oh == 0:
                e.wait_ge(b.sG.h, b.g_x1)
                e.wait_ge(b.sV.h, b.v_halo)
            else:
                e.wait_ge(b.sG.h, b._g_ff0)
            for k in range(KC):
                if k >= 1:
                    e.wait_ge(b.sV.h, v0 + (k - 1) * 4 + 4)
                b.dma(e, b.C4[:, 0:4096],
                      b.pT_s[k][:, oh * 4096:(oh + 1) * 4096], b.sW)
                b.dma(e, b.W2[:, 0:4096],
                      b.nT_s[k][:, oh * 4096:(oh + 1) * 4096], b.sW)
        blk.sync(sync_pn)

        def scalar_pn(e, w0=w0, v0=v0):
            for k in range(KC):
                e.wait_ge(b.sW.h, w0 + 32 * (k + 1))
                if k >= 1:
                    e.wait_ge(b.sV.h, v0 + (k - 1) * 4 + 4)
                e.activation(out=b.C4[:, 0:4096], in_=b.C4[:, 0:4096],
                             func=AF.Sigmoid).then_inc(b.sS.h, 1)
                b.sS.n += 1
                e.activation(out=b.W2[:, 0:4096], in_=b.W2[:, 0:4096],
                             func=AF.Sigmoid).then_inc(b.sS.h, 1)
                b.sS.n += 1
        blk.scalar(scalar_pn)

        def vector_comb(e, s0=s0):
            for k in range(KC):
                e.wait_ge(b.sS.h, s0 + k * 2 + 2)
                e.tensor_tensor(out=b.C4[:, 0:4096], in0=b.C4[:, 0:4096],
                                in1=b.W2[:, 0:4096], op=OP.subtract).then_inc(b.sV.h, 1)
                b.sV.n += 1
                e.drain()
                sp = b.C4[:, 0:4096].rearrange("p (o n) -> p o n", n=4)
                e.tensor_scalar(out=b.a_(k), in0=sp[:, :, 0],
                                scalar1=b.sm("cvec", 4)[:, 0:1], scalar2=None,
                                op0=OP.mult).then_inc(b.sV.h, 1)
                b.sV.n += 1
                for n_ in range(1, 4):
                    inc = (n_ == 3)
                    ins = e.scalar_tensor_tensor(out=b.a_(k), in0=sp[:, :, n_],
                                                 scalar=b.sm("cvec", 4)[:, n_:n_ + 1],
                                                 in1=b.a_(k), op0=OP.mult, op1=OP.add)
                    if inc:
                        ins.then_inc(b.sV.h, 1)
                        b.sV.n += 1
                e.engine_nop().then_inc(b.sV.h, 1)
                b.sV.n += 1
        blk.vector(vector_comb)
        v_wt = b.sV.n

        t0 = b.sT.n
        s1 = b.sS.n

        def tensor_ff(e, v_wt=v_wt, s1=s1, t0=t0):
            e.wait_ge(b.sV.h, v_wt)
            for mo in range(8):
                for g in range(NG):
                    psi = (mo % 2) * 2 + g
                    if mo >= 2:
                        e.wait_ge(b.sS.h, s1 + (mo - 2) * NG + g + 1)
                    ins = None
                    for k in range(KC):
                        ins = e.matmul(b.PS[psi][:, :NT],
                                       b.a_(k)[:, mo * 128:(mo + 1) * 128],
                                       b.b_(k)[:, g * NT:(g + 1) * NT],
                                       start=(k == 0), stop=(k == KC - 1))
                    ins.then_inc(b.sT.h, 1)
                    b.sT.n += 1
        blk.tensor(tensor_ff)

        G1 = b.sG.n

        def scalar_ff(e, oh=oh, t0=t0, G1=G1, s1=s1):
            for mo in range(8):
                gmo = oh * 8 + mo
                for g in range(NG):
                    e.wait_ge(b.sT.h, t0 + mo * NG + g + 1)
                    idx = mo * NG + g
                    if idx >= 4:
                        e.wait_ge(b.sG.h, G1 + 16 * (idx - 3))
                    psi = (mo % 2) * 2 + g
                    slot = 2 * T + (idx % 4) * NT
                    e.activation(out=b.C4[:, slot:slot + NT], in_=b.PS[psi][:, :NT],
                                 func=AF.Silu,
                                 bias=b.sm("bias1", KC)[:, gmo:gmo + 1]).then_inc(b.sS.h, 1)
                    b.sS.n += 1
        blk.scalar(scalar_ff)

        def gp_ff(e, oh=oh, s1=s1):
            for mo in range(8):
                gmo = oh * 8 + mo
                for g in range(NG):
                    idx = mo * NG + g
                    e.wait_ge(b.sS.h, s1 + idx + 1)
                    slot = 2 * T + (idx % 4) * NT
                    b.dma(e, b.d_gl1[gmo][:, g * NT:(g + 1) * NT],
                          b.C4[:, slot:slot + NT], b.sG)
        blk.gpsimd(gp_ff)
        b._g_ff0 = b.sG.n
    b.g_ffn1 = b.sG.n
    b.t_ffn1 = b.sT.n
    b.s_ffn1 = b.sS.n


def emit_phase6_ssd(b):
    blk = b.blk
    LgC = 2
    s0 = b.sS.n

    def scalar_dt(e):
        e.wait_ge(b.sS.h, b.s_ffn1)
        e.wait_ge(b.sV.h, b.v_proj)
        e.activation(out=b.sm2("acum", T), in_=b.sm2("dt", T), func=AF.Exp,
                     bias=b.sm2("dtbias"))
        e.drain()
        e.activation(out=b.sm2("dt", T), in_=b.sm2("acum", T), func=AF.Ln,
                     bias=1.0).then_inc(b.sS.h, 1)
        b.sS.n += 1
    blk.scalar(scalar_dt)
    s_dt = b.sS.n

    def vector_da(e):
        e.wait_ge(b.sS.h, s_dt)
        e.wait_ge(b.sT.h, b.t_ffn1)
        e.tensor_scalar(out=b.C4[0:NH, 0:T], in0=b.sm2("dt", T),
                        scalar1=b.sm2("A"), scalar2=None, op0=OP.mult)
        e.drain()
        e.memset(b.sm2("zeros", CS), 0.0)
        e.drain()
        for c in range(NCH):
            e.tensor_tensor_scan(out=b.sm2("acum", T)[:, c * CS:(c + 1) * CS],
                                 data0=b.C4[0:NH, c * CS:(c + 1) * CS],
                                 data1=b.sm2("zeros", CS),
                                 initial=0.0, op0=OP.add, op1=OP.add)
        e.drain()
        for c in range(NCH):
            e.tensor_copy(out=b.sm2("dalast", 4)[:, c:c + 1],
                          in_=b.sm2("acum", T)[:, (c + 1) * CS - 1:(c + 1) * CS])
        e.engine_nop().then_inc(b.sV.h, 1)
        b.sV.n += 1
    blk.vector(vector_da)
    v_da = b.sV.n

    t0 = b.sT.n

    def tensor_tr(e):
        e.wait_ge(b.sV.h, v_da)
        for tc in range(8):
            if tc >= 2:
                e.wait_ge(b.sV.h, v_da + (tc - 1))
            e.transpose(b.PS[0 + tc % 2][0:128, 0:NH],
                        b.sm2("acum", T)[:, tc * 128:(tc + 1) * 128],
                        b.sm("ident", 128)[0:NH, 0:NH])
            ins = e.transpose(b.PS[2 + tc % 2][0:128, 0:NH],
                              b.sm2("dt", T)[:, tc * 128:(tc + 1) * 128],
                              b.sm("ident", 128)[0:NH, 0:NH])
            ins.then_inc(b.sT.h, 1)
            b.sT.n += 1
    blk.tensor(tensor_tr)
    t_tr = b.sT.n

    def vector_tr(e):
        for tc in range(8):
            e.wait_ge(b.sT.h, t_tr - 8 + tc + 1)
            e.tensor_copy(out=b.M2[:, tc * 64:(tc + 1) * 64],
                          in_=b.PS[0 + tc % 2][0:128, 0:NH])
            e.tensor_copy(out=b.M2[:, 512 + tc * 64:512 + (tc + 1) * 64],
                          in_=b.PS[2 + tc % 2][0:128, 0:NH]).then_inc(b.sV.h, 1)
            b.sV.n += 1
    blk.vector(vector_tr)
    v_tr = b.sV.n

    gdl = b.sG.n

    def gp_dl(e):
        e.wait_ge(b.sV.h, v_tr)
        for tc in range(8):
            e.dma_start(out=b.row1[0:1, tc * 64:(tc + 1) * 64],
                        in_=b.sm2("dalast", 4)[:, tc // 2:tc // 2 + 1],
                        allow_slow_non_contiguous=True).then_inc(b.sG.h, 16)
            b.sG.n += 16
    blk.gpsimd(gp_dl)
    g_dl = b.sG.n

    t1 = b.sT.n

    def tensor_dl(e):
        e.wait_ge(b.sG.h, g_dl)
        ins = e.matmul(b.PS[4][:, 0:512], b.ones1[:], b.row1[0:1, 0:512],
                       start=True, stop=True)
        ins = e.transpose(b.PS[5][0:1, 0:NH], b.sm2("dparam"),
                          b.sm("ident", 128)[0:NH, 0:NH])
        ins.then_inc(b.sT.h, 1)
        b.sT.n += 1
    blk.tensor(tensor_dl)
    t_dl = b.sT.n

    def vector_dl(e):
        e.wait_ge(b.sT.h, t_dl)
        e.tensor_tensor(out=b.M2[:, 1024:1536], in0=b.PS[4][:, 0:512],
                        in1=b.M2[:, 0:512], op=OP.subtract)
        e.tensor_copy(out=b.row2[0:1, 0:NH], in_=b.PS[5][0:1, 0:NH])
        e.tensor_copy(out=b.C4[:, 0:512], in_=b.PS[4][:, 0:512]).then_inc(b.sV.h, 1)
        b.sV.n += 1
    blk.vector(vector_dl)
    v_dl = b.sV.n

    def scalar_dl(e):
        e.wait_ge(b.sV.h, v_dl)
        e.activation(out=b.M2[:, 1024:1536], in_=b.M2[:, 1024:1536], func=AF.Exp)
        e.activation(out=b.B[:, 6144:6656], in_=b.C4[:, 0:512],
                     func=AF.Exp).then_inc(b.sS.h, 1)
        b.sS.n += 1
    blk.scalar(scalar_dl)
    s_dl = b.sS.n

    tdc = b.sT.n

    def tensor_dc(e):
        e.wait_ge(b.sV.h, v_dl)
        e.matmul(b.PS[6][0:NH, 0:32], b.ones1[:, 0:NH], b.row2[0:1, 0:NH:2],
                 start=True, stop=True)
        ins = e.matmul(b.PS[7][0:NH, 0:32], b.ones1[:, 0:NH], b.row2[0:1, 1:NH:2],
                       start=True, stop=True)
        ins.then_inc(b.sT.h, 1)
        b.sT.n += 1
    blk.tensor(tensor_dc)
    t_dc = b.sT.n

    def vector_dc(e):
        e.wait_ge(b.sT.h, t_dc)
        e.wait_ge(b.sS.h, s_dl)
        e.tensor_copy(out=b.sm("dcols", 32)[0:64, :], in_=b.PS[6][0:NH, 0:32])
        e.tensor_copy(out=b.sm("dcols", 32)[64:128, :], in_=b.PS[7][0:NH, 0:32])
        e.tensor_tensor(out=b.M2[:, 1536:2048], in0=b.M2[:, 512:1024],
                        in1=b.M2[:, 1024:1536], op=OP.mult).then_inc(b.sV.h, 1)
        b.sV.n += 1
    blk.vector(vector_dc)
    v_prep = b.sV.n

    g0 = b.sG.n
    v1 = b.sV.n
    s1 = b.sS.n

    st_counts = {}

    def gp_conv(e):
        e.wait_ge(b.sG.h, b.g_proj)
        e.wait_ge(b.sV.h, v_prep)
        for m in range(36):
            if m < 34:
                if m >= 2:
                    e.wait_ge(b.sS.h, s1 + (m - 2) + 1)
                b.dma(e, b.C4[:, (m % 2) * 1027 + 3:(m % 2) * 1027 + 3 + T],
                      b.d_xbc[m], b.sG)
            sm_ = m - 2
            if 0 <= sm_ < 32:
                e.wait_ge(b.sS.h, s1 + sm_ + 1)
                b.dma(e, b.d_xs[sm_],
                      b.C4[:, 2 * 1027 + (sm_ % 2) * T:2 * 1027 + (sm_ % 2 + 1) * T],
                      b.sG)
                st_counts[sm_] = b.sG.n
    blk.gpsimd(gp_conv)
    g_xs = b.sG.n

    ld_counts = {}
    st_counts_c = {}
    nn = g0
    for m in range(36):
        if m < 34:
            nn += 16
            ld_counts[m] = nn
        if 0 <= m - 2 < 32:
            nn += 16
            st_counts_c[m - 2] = nn

    def vector_conv(e):
        for m in range(34):
            e.wait_ge(b.sG.h, ld_counts[m])
            if m >= 2 and (m - 2) in st_counts_c:
                e.wait_ge(b.sG.h, st_counts_c[m - 2])
            base = (m % 2) * 1027
            xin = b.C4[:, base:base + 1027]
            e.tensor_copy(out=xin[:, 0:3],
                          in_=b.sm("halo_rx", 102)[:, m * 3:(m + 1) * 3])
            e.drain()
            acc = b.C4[:, 2 * 1027 + (m % 2) * T:2 * 1027 + (m % 2 + 1) * T]
            cw = b.sm("convw", 136).rearrange("p (m k) -> p m k", k=4)
            e.tensor_scalar(out=acc, in0=xin[:, 0:T], scalar1=cw[:, m, 0:1],
                            scalar2=b.sm("convb", 34)[:, m:m + 1],
                            op0=OP.mult, op1=OP.add)
            e.drain()
            for kk in range(1, 4):
                e.scalar_tensor_tensor(out=acc, in0=xin[:, kk:kk + T],
                                       scalar=cw[:, m, kk:kk + 1], in1=acc,
                                       op0=OP.mult, op1=OP.add)
                e.drain()
            e.engine_nop().then_inc(b.sV.h, 1)
            b.sV.n += 1
    blk.vector(vector_conv)

    def scalar_conv(e):
        for m in range(34):
            e.wait_ge(b.sV.h, v1 + m + 1)
            acc = b.C4[:, 2 * 1027 + (m % 2) * T:2 * 1027 + (m % 2 + 1) * T]
            if m < 32:
                e.activation(out=acc, in_=acc, func=AF.Silu).then_inc(b.sS.h, 1)
            elif m == 32:
                e.activation(out=b.B[:, 0:T], in_=acc, func=AF.Silu).then_inc(b.sS.h, 1)
            else:
                e.activation(out=b.B[:, T:2 * T], in_=acc, func=AF.Silu).then_inc(b.sS.h, 1)
            b.sS.n += 1
    blk.scalar(scalar_conv)
    s_conv_end = b.sS.n

    AcumT = lambda tcg, h: b.M2[:, tcg * 64 + h:tcg * 64 + h + 1]
    dtTc = lambda tcg, h: b.M2[:, 512 + tcg * 64 + h:512 + tcg * 64 + h + 1]
    ddTc = lambda tcg, h: b.M2[:, 1536 + tcg * 64 + h:1536 + tcg * 64 + h + 1]

    last_vx = {"n": 0}
    prev_yass_dma = {"n": 0}
    for c in range(NCH):
        tB = b.sT.n

        def tensor_btok(e, c=c, tB=tB):
            if c == 0:
                e.wait_ge(b.sS.h, s_conv_end)
                e.wait_ge(b.sG.h, g_xs)
            else:
                e.wait_ge(b.sV.h, last_vx["n"])
            for j in range(2):
                ins = e.transpose(b.PS[2 + j][0:128, 0:128],
                                  b.B[:, c * CS + j * 128:c * CS + (j + 1) * 128],
                                  b.sm("ident", 128))
                ins.then_inc(b.sT.h, 1)
                b.sT.n += 1
        blk.tensor(tensor_btok)
        t_btok = b.sT.n

        def vector_btok(e, t_btok=t_btok):
            e.wait_ge(b.sT.h, t_btok)
            e.tensor_copy(out=b.C4[:, 0:128], in_=b.PS[2][0:128, 0:128])
            e.tensor_copy(out=b.C4[:, 128:256],
                          in_=b.PS[3][0:128, 0:128]).then_inc(b.sV.h, 1)
            b.sV.n += 1
        blk.vector(vector_btok)
        v_btok = b.sV.n

        t_cbt0 = b.sT.n

        def tensor_cbt(e, c=c, v_btok=v_btok):
            e.wait_ge(b.sV.h, v_btok)
            for sc in range(2):
                ins = e.matmul(b.PS[sc][:, 0:CS],
                               b.B[:, c * CS + sc * 128:c * CS + sc * 128 + 128],
                               b.B[:, T + c * CS:T + (c + 1) * CS],
                               start=True, stop=True)
                ins.then_inc(b.sT.h, 1)
                b.sT.n += 1
        blk.tensor(tensor_cbt)
        t_cbt = b.sT.n

        def vector_cbt(e, t_cbt=t_cbt):
            e.wait_ge(b.sT.h, t_cbt)
            e.tensor_tensor(out=b.C4[:, 256:256 + CS], in0=b.PS[0][:, 0:CS],
                            in1=b.sm("mask0", CS), op=OP.mult)
            e.tensor_tensor(out=b.C4[:, 512:512 + CS], in0=b.PS[1][:, 0:CS],
                            in1=b.sm("mask1", CS), op=OP.mult).then_inc(b.sV.h, 1)
            b.sV.n += 1
        blk.vector(vector_cbt)
        v_cbt = b.sV.n

        for m in range(32):
            gx0 = b.sG.n
            vlast = last_vx["n"]

            def gp_x(e, c=c, m=m, vlast=vlast, v_cbt=v_cbt):
                e.wait_ge(b.sV.h, max(vlast, v_cbt) if m >= 2 else v_cbt)
                b.dma(e, b.C4[:, 768 + (m % 2) * CS:768 + (m % 2 + 1) * CS],
                      b.d_xs[m][:, c * CS:(c + 1) * CS], b.sG)
            blk.gpsimd(gp_x)
            g_x = b.sG.n

            t_x0 = b.sT.n

            def tensor_x(e, c=c, m=m, g_x=g_x, vlast=vlast):
                e.wait_ge(b.sG.h, g_x)
                e.wait_ge(b.sV.h, vlast)
                for j in range(2):
                    ins = e.transpose(b.PS[2 + j][0:128, 0:128],
                                      b.C4[:, 768 + (m % 2) * CS + j * 128:768 + (m % 2) * CS + (j + 1) * 128],
                                      b.sm("ident", 128))
                    ins.then_inc(b.sT.h, 1)
                    b.sT.n += 1
            blk.tensor(tensor_x)
            t_x = b.sT.n

            def vector_x(e, c=c, m=m, t_x=t_x):
                e.wait_ge(b.sT.h, t_x)
                for j in range(2):
                    tcg = 2 * c + j
                    for hh in range(2):
                        h = 2 * m + hh
                        e.tensor_scalar(
                            out=b.C4[:, 1280 + j * 128 + hh * 64:1280 + j * 128 + (hh + 1) * 64],
                            in0=b.PS[2 + j][0:128, hh * 64:(hh + 1) * 64],
                            scalar1=dtTc(tcg, h), scalar2=None, op0=OP.mult)
                        e.tensor_scalar(
                            out=b.C4[:, 1536 + j * 128 + hh * 64:1536 + j * 128 + (hh + 1) * 64],
                            in0=b.PS[2 + j][0:128, hh * 64:(hh + 1) * 64],
                            scalar1=ddTc(tcg, h), scalar2=None, op0=OP.mult)
                e.engine_nop().then_inc(b.sV.h, 1)
                b.sV.n += 1
            blk.vector(vector_x)
            v_x = b.sV.n

            for hh in range(2):
                h = 2 * m + hh

                def tensor_ax(e, c=c, h=h, hh=hh, v_x=v_x):
                    e.wait_ge(b.sV.h, v_x + hh * 4)
                    ins = e.matmul(b.PS[3][0:1, 0:CS],
                                   b.sm("ident", 128)[0:NH, h:h + 1],
                                   b.sm2("acum", T)[:, c * CS:(c + 1) * CS],
                                   start=True, stop=True)
                    ins.then_inc(b.sT.h, 1)
                    b.sT.n += 1
                blk.tensor(tensor_ax)
                t_ax = b.sT.n

                def vector_ax(e, t_ax=t_ax):
                    e.wait_ge(b.sT.h, t_ax)
                    e.tensor_copy(out=b.row2[0:1, 0:CS],
                                  in_=b.PS[3][0:1, 0:CS]).then_inc(b.sV.h, 1)
                    b.sV.n += 1
                blk.vector(vector_ax)
                v_ax = b.sV.n

                def tensor_alb(e, v_ax=v_ax):
                    e.wait_ge(b.sV.h, v_ax)
                    ins = e.matmul(b.PS[4][:, 0:CS], b.ones1[:],
                                   b.row2[0:1, 0:CS], start=True, stop=True)
                    ins.then_inc(b.sT.h, 1)
                    b.sT.n += 1
                blk.tensor(tensor_alb)
                t_alb = b.sT.n

                def vector_seg(e, c=c, h=h, t_alb=t_alb):
                    e.wait_ge(b.sT.h, t_alb)
                    for sc in range(2):
                        tcg = 2 * c + sc
                        e.tensor_scalar(out=b.C4[:, 2304 + sc * 128:2304 + (sc + 1) * 128] if False else b.C4[:, 2304:2304 + CS] if sc == 0 else b.C4[:, 2304 + 0:2304 + CS],
                                        in0=b.PS[4][:, 0:CS],
                                        scalar1=AcumT(tcg, h), scalar2=0.0,
                                        op0=OP.subtract, op1=OP.min) if False else None
                    e.tensor_scalar(out=b.C4[:, 2304:2304 + CS],
                                    in0=b.PS[4][:, 0:CS],
                                    scalar1=AcumT(2 * c, h), scalar2=0.0,
                                    op0=OP.subtract, op1=OP.min)
                    e.tensor_scalar(out=b.C4[:, 1792:1792 + CS],
                                    in0=b.PS[4][:, 0:CS],
                                    scalar1=AcumT(2 * c + 1, h), scalar2=0.0,
                                    op0=OP.subtract, op1=OP.min).then_inc(b.sV.h, 1)
                    b.sV.n += 1
                blk.vector(vector_seg)
                v_seg = b.sV.n

                def scalar_seg(e, v_seg=v_seg):
                    e.wait_ge(b.sV.h, v_seg)
                    e.activation(out=b.C4[:, 2304:2304 + CS],
                                 in_=b.C4[:, 2304:2304 + CS], func=AF.Exp)
                    e.activation(out=b.C4[:, 1792:1792 + CS],
                                 in_=b.C4[:, 1792:1792 + CS],
                                 func=AF.Exp).then_inc(b.sS.h, 1)
                    b.sS.n += 1
                blk.scalar(scalar_seg)
                s_seg = b.sS.n

                def vector_mh(e, s_seg=s_seg):
                    e.wait_ge(b.sS.h, s_seg)
                    e.tensor_tensor(out=b.C4[:, 2304:2304 + CS],
                                    in0=b.C4[:, 2304:2304 + CS],
                                    in1=b.C4[:, 256:256 + CS], op=OP.mult)
                    e.tensor_tensor(out=b.C4[:, 1792:1792 + CS],
                                    in0=b.C4[:, 1792:1792 + CS],
                                    in1=b.C4[:, 512:512 + CS],
                                    op=OP.mult).then_inc(b.sV.h, 1)
                    b.sV.n += 1
                blk.vector(vector_mh)
                v_mh = b.sV.n

                t_y0 = b.sT.n

                def tensor_yd(e, c=c, m=m, hh=hh, v_mh=v_mh):
                    e.wait_ge(b.sV.h, v_mh)
                    for j, mhoff in ((0, 2304), (1, 1792)):
                        e.matmul(b.PS[5][0:64, 0:CS],
                                 b.C4[:, 1280 + j * 128 + hh * 64:1280 + j * 128 + (hh + 1) * 64],
                                 b.C4[:, mhoff:mhoff + CS],
                                 start=(j == 0), stop=(j == 1))
                    ins = None
                    for j in range(2):
                        ins = e.matmul(b.PS[6][:, 0:64],
                                       b.C4[:, j * 128:(j + 1) * 128],
                                       b.C4[:, 1536 + j * 128 + hh * 64:1536 + j * 128 + (hh + 1) * 64],
                                       start=(j == 0), stop=(j == 1))
                    ins.then_inc(b.sT.h, 1)
                    b.sT.n += 1
                blk.tensor(tensor_yd)
                t_yd = b.sT.n

                def vector_ev(e, c=c, m=m, hh=hh, h=h, t_yd=t_yd):
                    e.wait_ge(b.sT.h, t_yd)
                    if hh == 0 and prev_yass_dma["n"]:
                        e.wait_ge(b.sG.h, prev_yass_dma["n"])
                    e.tensor_copy(out=b.C4[hh * 64:(hh + 1) * 64, 2560:2560 + CS],
                                  in_=b.PS[5][0:64, 0:CS])
                    e.tensor_copy(out=b.A[:, c * 4096 + h * 64:c * 4096 + (h + 1) * 64],
                                  in_=b.PS[6][:, 0:64]).then_inc(b.sV.h, 1)
                    b.sV.n += 1
                blk.vector(vector_ev)
            last_vx["n"] = b.sV.n

            def gp_yass(e, c=c, m=m, vn=b.sV.n):
                e.wait_ge(b.sV.h, vn)
                b.dma(e, b.d_yp[m][:, c * CS:(c + 1) * CS],
                      b.C4[:, 2560:2560 + CS], b.sG)
            blk.gpsimd(gp_yass)
            prev_yass_dma["n"] = b.sG.n

    g_sw1 = b.sG.n
    v_sw1 = last_vx["n"]

    def vector_chain(e):
        e.wait_ge(b.sV.h, v_sw1)
        S = b.B[:, 2048:6144]
        e.memset(S, 0.0)
        e.drain()
        for c in range(NCH):
            for h in range(NH):
                e.tensor_scalar(out=S[:, h * 64:(h + 1) * 64],
                                in0=S[:, h * 64:(h + 1) * 64],
                                scalar1=b.B[:, 6144 + (2 * c) * 64 + h:6144 + (2 * c) * 64 + h + 1],
                                scalar2=None, op0=OP.mult)
            e.drain()
            e.tensor_tensor(out=S, in0=S, in1=b.A[:, c * 4096:(c + 1) * 4096],
                            op=OP.add)
            e.drain()
        e.engine_nop().then_inc(b.sV.h, 1)
        b.sV.n += 1
    blk.vector(vector_chain)
    v_chain = b.sV.n

    cc0 = b.sC.n
    gb0 = b.sG.n

    def gp_cc3(e):
        e.wait_ge(b.sV.h, v_chain)
        e.wait_ge(b.sG.h, g_sw1)
        b.dma(e, b.cc3_src, b.B[:, 2048:6144], b.sG)
        e.wait_ge(b.sG.h, gb0 + 16)
        e.collective_compute(
            "AllGather", OP.bypass, replica_groups=[list(range(8))],
            ins=[b.cc3_src.opt()], outs=[b.cc3_dst.opt()],
        ).then_inc(b.sC.h, 1)
        b.sC.n += 1
        e.wait_ge(b.sC.h, cc0 + 1)
    blk.gpsimd(gp_cc3)
    g_cc3 = b.sG.n

    v_sin0 = b.sV.n
    g_sin0 = b.sG.n

    def gp_sin(e):
        for r in range(8):
            if r >= 1:
                e.wait_ge(b.sV.h, v_sin0 + r)
            b.dma(e, b.C4[:, 0:4096], b.cc3_dst[r], b.sG)
    blk.gpsimd(gp_sin)

    def vector_sin(e):
        S = b.B[:, 2048:6144]
        e.wait_ge(b.sG.h, g_sin0)
        e.memset(S, 0.0)
        e.drain()
        for r in range(8):
            e.wait_ge(b.sG.h, g_sin0 + 16 * (r + 1))
            e.scalar_tensor_tensor(out=S, in0=b.C4[:, 0:4096],
                                   scalar=b.sm("sel", 8)[:, r:r + 1], in1=S,
                                   op0=OP.mult, op1=OP.add).then_inc(b.sV.h, 1)
            b.sV.n += 1
            e.drain()
    blk.vector(vector_sin)
    v_sin = b.sV.n

    g_fence = {"yd": 0, "x": 0, "z": 0}
    prev_tss = {"n": 0}
    last_v = {"n": v_sin}
    last_g1dma = {"n": 0}
    ss_written = {"n": 0}
    for c in range(NCH):
        for m in range(32):
            gy0 = b.sG.n
            vprev = last_v["n"]

            def gp_in2(e, c=c, m=m, vprev=vprev):
                e.wait_ge(b.sV.h, vprev)
                if c == 0 and m == 0:
                    e.wait_ge(b.sG.h, g_sw1)
                b.dma(e, b.C4[:, (m % 2) * CS:(m % 2 + 1) * CS],
                      b.d_yp[m][:, c * CS:(c + 1) * CS], b.sG)
                b.dma(e, b.C4[:, 512 + (m % 2) * CS:512 + (m % 2 + 1) * CS],
                      b.d_xs[m][:, c * CS:(c + 1) * CS], b.sG)
                b.dma(e, b.C4[:, 1024 + (m % 2) * CS:1024 + (m % 2 + 1) * CS],
                      b.d_z[m][:, c * CS:(c + 1) * CS], b.sG)
            blk.gpsimd(gp_in2)
            g_in2 = b.sG.n

            for hh in range(2):
                h = 2 * m + hh

                def tensor_yo(e, c=c, h=h, hh=hh, g_in2=g_in2, vprev=vprev):
                    if hh == 0:
                        e.wait_ge(b.sG.h, g_in2)
                        e.wait_ge(b.sV.h, vprev)
                    else:
                        e.wait_ge(b.sV.h, vprev + 2)
                    e.matmul(b.PS[5][0:64, 0:CS],
                             b.B[:, 2048 + h * 64:2048 + (h + 1) * 64],
                             b.B[:, T + c * CS:T + (c + 1) * CS],
                             start=True, stop=True)
                    ins = e.matmul(b.PS[3][0:1, 0:CS],
                                   b.sm("ident", 128)[0:NH, h:h + 1],
                                   b.sm2("acum", T)[:, c * CS:(c + 1) * CS],
                                   start=True, stop=True)
                    ins.then_inc(b.sT.h, 1)
                    b.sT.n += 1
                blk.tensor(tensor_yo)
                t_yo = b.sT.n

                def vector_ax2(e, t_yo=t_yo):
                    e.wait_ge(b.sT.h, t_yo)
                    e.tensor_copy(out=b.row2[0:1, 0:CS],
                                  in_=b.PS[3][0:1, 0:CS]).then_inc(b.sV.h, 1)
                    b.sV.n += 1
                blk.vector(vector_ax2)
                v_ax2 = b.sV.n

                def tensor_alb2(e, v_ax2=v_ax2):
                    e.wait_ge(b.sV.h, v_ax2)
                    ins = e.matmul(b.PS[4][:, 0:CS], b.ones1[:],
                                   b.row2[0:1, 0:CS], start=True, stop=True)
                    ins.then_inc(b.sT.h, 1)
                    b.sT.n += 1
                blk.tensor(tensor_alb2)
                t_alb2 = b.sT.n

                def scalar_ea(e, t_yo=t_alb2):
                    e.wait_ge(b.sT.h, t_yo)
                    e.activation(out=b.C4[0:64, 1536:1536 + CS],
                                 in_=b.PS[4][0:64, 0:CS],
                                 func=AF.Exp).then_inc(b.sS.h, 1)
                    b.sS.n += 1
                blk.scalar(scalar_ea)
                s_ea = b.sS.n

                def vector_yh(e, c=c, m=m, hh=hh, s_ea=s_ea):
                    e.wait_ge(b.sS.h, s_ea)
                    ya = b.C4[hh * 64:(hh + 1) * 64, 1792:1792 + CS]
                    e.tensor_tensor(out=ya, in0=b.PS[5][0:64, 0:CS],
                                    in1=b.C4[0:64, 1536:1536 + CS], op=OP.mult)
                    e.drain()
                    e.tensor_tensor(out=ya, in0=ya,
                                    in1=b.C4[hh * 64:(hh + 1) * 64, (m % 2) * CS:(m % 2 + 1) * CS],
                                    op=OP.add).then_inc(b.sV.h, 1)
                    b.sV.n += 1
                blk.vector(vector_yh)
            v_yh = b.sV.n

            def vector_dx(e, c=c, m=m, v_yh=v_yh):
                e.drain()
                e.scalar_tensor_tensor(out=b.C4[:, 1792:1792 + CS],
                                       in0=b.C4[:, 512 + (m % 2) * CS:512 + (m % 2 + 1) * CS],
                                       scalar=b.sm("dcols", 32)[:, m:m + 1],
                                       in1=b.C4[:, 1792:1792 + CS],
                                       op0=OP.mult, op1=OP.add).then_inc(b.sV.h, 1)
                b.sV.n += 1
            blk.vector(vector_dx)
            v_dx = b.sV.n

            def scalar_sz(e, c=c, m=m, v_dx=v_dx, g_in2=g_in2):
                e.wait_ge(b.sG.h, g_in2)
                e.activation(out=b.C4[:, 2048:2048 + CS],
                             in_=b.C4[:, 1024 + (m % 2) * CS:1024 + (m % 2 + 1) * CS],
                             func=AF.Silu).then_inc(b.sS.h, 1)
                b.sS.n += 1
            blk.scalar(scalar_sz)
            s_sz = b.sS.n

            def vector_g1(e, c=c, m=m, s_sz=s_sz, v_dx=v_dx):
                e.wait_ge(b.sS.h, s_sz)
                e.wait_ge(b.sV.h, v_dx)
                if last_g1dma["n"]:
                    e.wait_ge(b.sG.h, last_g1dma["n"])
                e.tensor_tensor(out=b.C4[:, 2304 + (m % 2) * CS:2304 + (m % 2 + 1) * CS],
                                in0=b.C4[:, 1792:1792 + CS],
                                in1=b.C4[:, 2048:2048 + CS],
                                op=OP.mult).then_inc(b.sV.h, 1)
                b.sV.n += 1
            blk.vector(vector_g1)
            v_g1 = b.sV.n

            def scalar_sq(e, c=c, m=m, v_g1=v_g1, pt=prev_tss["n"]):
                e.wait_ge(b.sV.h, v_g1)
                if pt:
                    e.wait_ge(b.sT.h, pt)
                e.activation(out=b.C4[:, 2816:2816 + CS],
                             in_=b.C4[:, 2304 + (m % 2) * CS:2304 + (m % 2 + 1) * CS],
                             func=AF.Square).then_inc(b.sS.h, 1)
                b.sS.n += 1
            blk.scalar(scalar_sq)
            s_sq = b.sS.n

            def tensor_ss(e, c=c, m=m, s_sq=s_sq):
                e.wait_ge(b.sS.h, s_sq)
                ins = e.matmul(b.PS[7][0:1, 0:CS], b.onesc[:],
                               b.C4[:, 2816:2816 + CS],
                               start=(m == 0), stop=(m == 31))
                ins.then_inc(b.sT.h, 1)
                b.sT.n += 1
            blk.tensor(tensor_ss)
            t_ssx = b.sT.n
            prev_tss["n"] = t_ssx

            def gp_g1(e, c=c, m=m, v_g1=v_g1, t_ssx=t_ssx):
                e.wait_ge(b.sV.h, v_g1)
                b.dma(e, b.d_g1[m][:, c * CS:(c + 1) * CS],
                      b.C4[:, 2304 + (m % 2) * CS:2304 + (m % 2 + 1) * CS], b.sG)
            blk.gpsimd(gp_g1)
            last_g1dma["n"] = b.sG.n
            last_v["n"] = b.sV.n
            if m == 31:
                t_ss_final = b.sT.n

                def vector_ssrow(e, c=c, t_ss_final=t_ss_final):
                    e.wait_ge(b.sT.h, t_ss_final)
                    e.tensor_copy(out=b.row1[0:1, c * CS:(c + 1) * CS],
                                  in_=b.PS[7][0:1, 0:CS]).then_inc(b.sV.h, 1)
                    b.sV.n += 1
                blk.vector(vector_ssrow)
                last_v["n"] = b.sV.n
        vS0 = last_v["n"]

        def vector_su(e, c=c, vS0=vS0):
            e.wait_ge(b.sV.h, vS0)
            S = b.B[:, 2048:6144]
            for h in range(NH):
                e.tensor_scalar(out=S[:, h * 64:(h + 1) * 64],
                                in0=S[:, h * 64:(h + 1) * 64],
                                scalar1=b.B[:, 6144 + (2 * c) * 64 + h:6144 + (2 * c) * 64 + h + 1],
                                scalar2=None, op0=OP.mult)
            e.drain()
            e.tensor_tensor(out=S, in0=S, in1=b.A[:, c * 4096:(c + 1) * 4096],
                            op=OP.add)
            e.drain()
            e.engine_nop().then_inc(b.sV.h, 1)
            b.sV.n += 1
        blk.vector(vector_su)
        last_v["n"] = b.sV.n
    b.v_sw2 = last_v["n"]
    b.g_sw2 = b.sG.n


def emit_phase7_out(b):
    blk = b.blk
    def vector_rs7(e):
        e.wait_ge(b.sV.h, b.v_sw2)
        e.tensor_scalar(out=b.row2[:], in0=b.row1[:], scalar1=1.0 / DIN,
                        scalar2=EPS, op0=OP.mult, op1=OP.add)
        e.drain()
        e.reciprocal(out=b.row1[:], in_=b.row2[:]).then_inc(b.sV.h, 1)
        b.sV.n += 1
    blk.vector(vector_rs7)
    v_r7 = b.sV.n

    def scalar_rs7(e):
        e.wait_ge(b.sV.h, v_r7)
        e.activation(out=b.row1[:], in_=b.row1[:], func=AF.Sqrt).then_inc(b.sS.h, 1)
        b.sS.n += 1
    blk.scalar(scalar_rs7)
    s_r7 = b.sS.n

    def vector_nt7(e):
        e.wait_ge(b.sS.h, s_r7)
        tmp = b.M2[0:1, 0:T]
        e.tensor_tensor(out=tmp, in0=b.row1[:], in1=b.row1[:], op=OP.mult)
        e.drain()
        e.tensor_tensor(out=tmp, in0=tmp, in1=b.row2[:], op=OP.mult)
        e.drain()
        e.tensor_scalar(out=tmp, in0=tmp, scalar1=-0.5, scalar2=1.5,
                        op0=OP.mult, op1=OP.add)
        e.drain()
        e.tensor_tensor(out=b.row1[:], in0=b.row1[:], in1=tmp,
                        op=OP.mult).then_inc(b.sV.h, 1)
        b.sV.n += 1
    blk.vector(vector_nt7)
    v_n7 = b.sV.n

    t_b7 = b.sT.n

    def tensor_b7(e):
        e.wait_ge(b.sV.h, v_n7)
        ins = None
        for g in range(NG):
            ins = e.matmul(b.PS[4 + g][:, :NT], b.ones1[:],
                           b.row1[0:1, g * NT:(g + 1) * NT], start=True, stop=True)
        ins.then_inc(b.sT.h, 1)
        b.sT.n += 1
    blk.tensor(tensor_b7)
    t_bc7 = b.sT.n

    def vector_b7(e):
        e.wait_ge(b.sT.h, t_bc7)
        ins = None
        for g in range(NG):
            ins = e.tensor_copy(out=b.M2[:, g * NT:(g + 1) * NT], in_=b.PS[4 + g][:, :NT])
        ins.then_inc(b.sV.h, 1)
        b.sV.n += 1
    blk.vector(vector_b7)
    v_b7 = b.sV.n

    out_t = b.out.rearrange("(k p) t -> k p t", p=128)
    for tt in range(2):
        g0 = b.sG.n
        v0 = b.sV.n

        def gp_gn(e, tt=tt, v0=v0):
            if tt == 0:
                e.wait_ge(b.sV.h, v_b7)
                e.wait_ge(b.sG.h, b.g_sw2)
            else:
                e.wait_ge(b.sT.h, b._t_p7)
            for k in range(32):
                if k >= 4:
                    e.wait_ge(b.sV.h, v0 + (k - 4) + 1)
                b.dma(e, b.C4[:, (k % 4) * NT:(k % 4 + 1) * NT],
                      b.d_g1[k][:, tt * NT:(tt + 1) * NT], b.sG)
        blk.gpsimd(gp_gn)

        def vector_gn(e, tt=tt, g0=g0):
            for k in range(32):
                e.wait_ge(b.sG.h, g0 + 16 * (k + 1))
                e.scalar_tensor_tensor(out=b.B[:, k * NT:(k + 1) * NT],
                                       in0=b.C4[:, (k % 4) * NT:(k % 4 + 1) * NT],
                                       scalar=b.sm("normw", 32)[:, k:k + 1],
                                       in1=b.M2[:, tt * NT:(tt + 1) * NT],
                                       op0=OP.mult, op1=OP.mult).then_inc(b.sV.h, 1)
                b.sV.n += 1
        blk.vector(vector_gn)
        v_gn = b.sV.n

        eb = b.sV.n
        t_p7 = emit_mm_dxd(b, lambda mo: b.wout_s[mo],
                           lambda k, g: b.B[:, k * NT:(k + 1) * NT],
                           (b.sV, eb), mo_list=list(range(KC)), nk=32,
                           ntok=NT, ngroups=1,
                           extra_tensor_waits=((b.sV, v_gn),),
                           wbufs=(b.W2[:, 0:4096], b.W4[:, 0:4096]))
        b._t_p7 = t_p7 + KC

        gl0 = b.sG.n
        ld2 = {}
        st2 = {}
        nn2 = gl0
        for mo in range(KC + 2):
            if mo < KC:
                nn2 += 32
                ld2[mo] = nn2
            if 0 <= mo - 2 < KC:
                nn2 += 16
                st2[mo - 2] = nn2

        def gp_r7(e, tt=tt, eb=eb, vv0=None):
            for mo in range(KC + 2):
                if mo < KC:
                    if mo >= 2:
                        e.wait_ge(b.sV.h, eb + (mo - 2) + 1)
                    b.dma(e, b.C4[:, 2048 + (mo % 2) * NT:2048 + (mo % 2 + 1) * NT],
                          b.d_x1[mo][:, tt * NT:(tt + 1) * NT], b.sG)
                    b.dma(e, b.C4[:, 3072 + (mo % 2) * NT:3072 + (mo % 2 + 1) * NT],
                          b.d_gl1[mo][:, tt * NT:(tt + 1) * NT], b.sG)
                so = mo - 2
                if 0 <= so < KC:
                    e.wait_ge(b.sV.h, eb + so + 1)
                    b.dma(e, out_t[so][:, tt * NT:(tt + 1) * NT],
                          b.C4[:, 3072 + (so % 2) * NT:3072 + (so % 2 + 1) * NT],
                          b.sG)
        blk.gpsimd(gp_r7)

        def vector_x2(e, tt=tt, t_p7=t_p7, eb=eb):
            for mo in range(KC):
                e.wait_ge(b.sT.h, t_p7 + mo + 1)
                e.wait_ge(b.sG.h, ld2[mo])
                if mo >= 2:
                    e.wait_ge(b.sG.h, st2[mo - 2])
                psi = (mo % 2) * 2
                e.tensor_tensor(out=b.C4[:, 3072 + (mo % 2) * NT:3072 + (mo % 2 + 1) * NT],
                                in0=b.PS[psi][:, :NT],
                                in1=b.C4[:, 3072 + (mo % 2) * NT:3072 + (mo % 2 + 1) * NT],
                                op=OP.mult)
                e.drain()
                e.tensor_tensor(out=b.C4[:, 3072 + (mo % 2) * NT:3072 + (mo % 2 + 1) * NT],
                                in0=b.C4[:, 3072 + (mo % 2) * NT:3072 + (mo % 2 + 1) * NT],
                                in1=b.C4[:, 2048 + (mo % 2) * NT:2048 + (mo % 2 + 1) * NT],
                                op=OP.add).then_inc(b.sV.h, 1)
                b.sV.n += 1
        blk.vector(vector_x2)


def build_program(stage="full"):
    b = Builder(stage)
    emit_preamble(b)
    emit_phase1_ln0(b)
    if stage == "l0cw":
        pass
    else:
        emit_phase2_attn(b, upto=("h" if stage in ("l0h", "l0f") else "all"))
    if stage == "l0cw":
        def gp_cw(e):
            e.wait_ge(b.sV.h, b.sV.n)
            e.wait_ge(b.sS.h, b.sS.n)
            e.wait_ge(b.sT.h, b.sT.n)
            e.wait_ge(b.sG.h, b.sG.n)
            b.dma(e, b.out[0:E, :], b.cwrow, b.sG)
            e.wait_ge(b.sG.h, b.sG.n)
        b.blk.gpsimd(gp_cw)
        finish(b)
        return b
    if stage.startswith("dump:"):
        emit_debug_out(b, getattr(b, stage.split(":")[1]))
        finish(b)
        return b
    if stage == "l0s":
        emit_debug_out(b, b.d_s)
        finish(b)
        return b
    if stage in ("l0h", "l0f", "l0u2"):
        src_buf = b.B if stage == "l0f" else b.A
        def gp(e):
            e.wait_ge(b.sV.h, b.sV.n)
            e.wait_ge(b.sS.h, b.sS.n)
            e.wait_ge(b.sT.h, b.sT.n)
            e.wait_ge(b.sG.h, b.sG.n)
            out_t = b.out.rearrange("(k p) t -> k p t", p=128)
            for k in range(KC):
                b.dma(e, out_t[k], src_buf[:, k * T:(k + 1) * T], b.sG)
            e.wait_ge(b.sG.h, b.sG.n)
        b.blk.gpsimd(gp)
        finish(b)
        return b
    emit_phase3_moe(b)
    if stage == "l0g":
        emit_debug_out(b, b.d_g)
        finish(b)
        return b
    emit_phase4_update(b)
    if stage == "l0":
        emit_debug_out(b, b.d_x1)
        finish(b)
        return b
    emit_phase5_proj(b)
    emit_phase5b_ffn1(b)
    emit_phase6_ssd(b)
    if stage.startswith("dumpL:"):
        _, name, off = stage.split(":")
        emit_debug_big(b, getattr(b, name), int(off))
        finish(b)
        return b
    emit_phase7_out(b)
    finish(b)
    return b



_STAGE = "full"


def kernel(**inputs):
    shared, percore = prep_inputs(inputs)
    b = build_program(_STAGE)
    in_maps = []
    for c in range(8):
        m = dict(shared)
        m.update(percore[c])
        in_maps.append(m)
    res = run_bass_kernel_spmd(b.nc, in_maps, core_ids=list(range(8)))
    x = np.asarray(inputs["L0_attn_x"], np.float32)
    out = np.empty_like(x)
    for c in range(8):
        bb, hh = c // 2, c % 2
        out[bb, hh * T:(hh + 1) * T, :] = res.results[c]["out"].T
    return out
